# version 02
# baseline (speedup 1.0000x reference)
import numpy as np
import ml_dtypes
import concourse.bass as bass
import concourse.mybir as mybir
from concourse.bass_utils import run_bass_kernel_spmd
from concourse.bass_core_rust import IndirectOffsetOnAxis

F32 = mybir.dt.float32
BF16 = mybir.dt.bfloat16
I32 = mybir.dt.int32
AF = mybir.ActivationFunctionType
ALU = mybir.AluOpType
Bft = ml_dtypes.bfloat16

NC = 8
T = 8192
D = 2048
DSH = D // NC
NTILE = DSH // 128
DI1 = 4096
DSH1 = DI1 // NC
NTILE1 = DSH1 // 128
DN = 16
NE = 8
CAP = 1280
TC = 512
NCH = T // TC
ISH = 8192 // NC
NCTRL = 130156
EPS = 1e-5
BIGS = float(1 << 20)
SENT = 1 << 30


_GRID_CACHE = {}


def _spline_grids():
    n = 8192 * 2048
    t = np.linspace(0.0, NCTRL - 1.0, n, dtype=np.float64)
    i = np.clip(np.floor(t).astype(np.int64), 0, NCTRL - 2)
    f = (t - i).astype(np.float32).astype(np.float64)
    idx = [np.clip(i + k, 0, NCTRL - 1).astype(np.int64) for k in (-1, 0, 1, 2)]
    f2 = f * f
    f3 = f2 * f
    coefs = [(0.5 * (-f + 2 * f2 - f3)).astype(np.float32),
             (0.5 * (2.0 - 5 * f2 + 3 * f3)).astype(np.float32),
             (0.5 * (f + 4 * f2 - 3 * f3)).astype(np.float32),
             (0.5 * (-f2 + f3)).astype(np.float32)]
    return idx, coefs


def _grids():
    if "g" not in _GRID_CACHE:
        _GRID_CACHE["g"] = _spline_grids()
    return _GRID_CACHE["g"]


def _prep_core(c, inp):
    f32 = lambda a: np.asarray(a, np.float32)
    d = {}
    x = f32(inp["L0_attn_x"][0])
    d["x_sh"] = np.ascontiguousarray(x.T[c * DSH:(c + 1) * DSH])
    d["w_gateT_sh"] = np.ascontiguousarray(
        f32(inp["L0_ffn_gate_weight"]).T[c * DSH:(c + 1) * DSH])
    d["w_xprojT_sh"] = np.ascontiguousarray(
        f32(inp["L0_attn_W_xproj"]).T[c * DSH:(c + 1) * DSH])
    d["w_dtT_sh"] = np.ascontiguousarray(
        f32(inp["L0_attn_W_dt"]).T[:, c * DSH:(c + 1) * DSH])
    d["b_dt_sh"] = np.ascontiguousarray(
        f32(inp["L0_attn_b_dt"])[c * DSH:(c + 1) * DSH]).reshape(DSH, 1)
    d["A0_sh"] = np.ascontiguousarray(f32(inp["L0_attn_A_log"])[c * DSH:(c + 1) * DSH])
    d["D0_sh"] = np.ascontiguousarray(f32(inp["L0_attn_D"])[c * DSH:(c + 1) * DSH]).reshape(DSH, 1)
    d["w1_e"] = np.ascontiguousarray(f32(inp["L0_ffn_w1"][c]))
    d["w2_e"] = np.ascontiguousarray(f32(inp["L0_ffn_w2"][c]))
    wip = f32(inp["L1_attn_in_proj_weight"])
    cols = np.concatenate([np.arange(c * DSH1, (c + 1) * DSH1),
                           np.arange(DI1 + c * DSH1, DI1 + (c + 1) * DSH1)])
    d["w_inprojT_sh"] = np.ascontiguousarray(wip[cols].T)
    d["conv_sh"] = np.ascontiguousarray(f32(inp["L1_attn_conv1d_weight"])[c * DSH1:(c + 1) * DSH1])
    d["w_xprojT1_sh"] = np.ascontiguousarray(
        f32(inp["L1_attn_x_proj_weight"]).T[c * DSH1:(c + 1) * DSH1])
    d["w_dtprojT1_sh"] = np.ascontiguousarray(
        f32(inp["L1_attn_dt_proj_weight"]).T[:, c * DSH1:(c + 1) * DSH1])
    d["b_dt1_sh"] = np.ascontiguousarray(
        f32(inp["L1_attn_dt_proj_bias"])[c * DSH1:(c + 1) * DSH1]).reshape(DSH1, 1)
    d["A1_sh"] = np.ascontiguousarray(f32(inp["L1_attn_A_log"])[c * DSH1:(c + 1) * DSH1])
    d["D1_sh"] = np.ascontiguousarray(f32(inp["L1_attn_D"])[c * DSH1:(c + 1) * DSH1]).reshape(DSH1, 1)
    d["w_outprojT_sh"] = np.ascontiguousarray(
        f32(inp["L1_attn_out_proj_weight"]).T[c * DSH1:(c + 1) * DSH1])
    idx, coefs = _grids()
    half_, ht_, p_, f_ = np.meshgrid(np.arange(2), np.arange(16), np.arange(128),
                                     np.arange(512), indexing="ij")
    j_gu = (ISH * c + 512 * half_ + f_) * 2048 + (128 * ht_ + p_)
    half2_, it_, p2_, h_ = np.meshgrid(np.arange(2), np.arange(4), np.arange(128),
                                       np.arange(2048), indexing="ij")
    j_d = h_ * 8192 + (ISH * c + 512 * half2_ + 128 * it_ + p2_)
    gcp = f32(inp["L1_ffn_gate_cp"])
    ucp = f32(inp["L1_ffn_up_cp"])
    dcp = f32(inp["L1_ffn_down_cp"])
    for k in range(4):
        gk = idx[k][j_gu]
        dk = idx[k][j_d]
        d[f"st_g{k}"] = gcp[gk]
        d[f"st_u{k}"] = ucp[gk]
        d[f"st_d{k}"] = dcp[dk]
        d[f"cf_g{k}"] = coefs[k][j_gu].astype(Bft)
        d[f"cf_d{k}"] = coefs[k][j_d].astype(Bft)
    d["ident"] = np.eye(128, dtype=Bft)
    d["identf"] = np.eye(128, dtype=np.float32)
    seln = np.zeros((16, 16 * 128), dtype=Bft)
    for n in range(16):
        seln[n, n * 128:(n + 1) * 128] = 1.0
    d["seln"] = seln
    tok = (np.arange(64)[None, :] * 128 + np.arange(128)[:, None]).astype(np.int32)
    d["tokidf"] = np.ascontiguousarray(tok.astype(np.float32))
    d["iota1280"] = np.broadcast_to(np.arange(CAP, dtype=np.float32), (128, CAP)).copy()
    d["myeoh"] = np.zeros((128, 8), np.float32)
    d["myeoh"][:, c] = 1.0
    return d


def prep_inputs(inputs):
    return [_prep_core(c, inputs) for c in range(NC)]



class Sem:
    __slots__ = ("h", "n", "_name")

    def __init__(self, name):
        self.h = None
        self.n = 0
        self._name = name

    def inc(self, k):
        self.n += k
        return self.n


class Q:
    def __init__(self, bld, key):
        self.b = bld
        self.key = key

    def op(self, name, *a, inc=None, **kw):
        if inc is not None:
            inc[0].inc(inc[1])

        import traceback
        def f(eng, name=name, a=a, kw=kw, inc=inc):
            try:
                ins = getattr(eng, name)(*a, **kw)
            except Exception:
                print(f"FAILED op {name} kw={list(kw.keys())} a={a}")
                raise
            if inc is not None:
                ins.then_inc(inc[0].h, inc[1])
        self.b.ops[self.key].append(f)

    def wait(self, sem, val=None):
        if val is None:
            val = sem.n

        def f(eng, sem=sem, val=val):
            eng.wait_ge(sem.h, val)
        self.b.ops[self.key].append(f)


class Bld:
    def __init__(self, nc):
        self.nc = nc
        self.ops = {"v": [], "s": [], "t": [], "g": [], "y": []}
        self.v = Q(self, "v")
        self.s = Q(self, "s")
        self.t = Q(self, "t")
        self.g = Q(self, "g")
        self.y = Q(self, "y")
        self.sems = []

    def sem(self, name):
        s = Sem(name)
        self.sems.append(s)
        return s


def _finalize(nc, b):
    import contextlib
    with contextlib.ExitStack() as st:
        for s in b.sems:
            s.h = st.enter_context(nc.semaphore(s._name))
        blk = st.enter_context(nc.Block())
        blk.vector(lambda e: [f(e) for f in b.ops["v"]])
        blk.scalar(lambda e: [f(e) for f in b.ops["s"]])
        blk.tensor(lambda e: [f(e) for f in b.ops["t"]])
        blk.gpsimd(lambda e: [f(e) for f in b.ops["g"]])
        blk.sync(lambda e: [f(e) for f in b.ops["y"]])
    return nc



def build(upto=99, taps=()):
    nc = bass.Bass("TRN2", target_bir_lowering=False, debug=False)
    b = Bld(nc)
    taps = set(taps)

    def ein(name, shape, dt=F32):
        return nc.dram_tensor(name, shape, dt, kind="ExternalInput").ap()

    def dint(name, shape, dt=F32):
        return nc.dram_tensor(name, shape, dt, kind="Internal").ap()

    x_sh = ein("x_sh", [DSH, T])
    w_gateT_sh = ein("w_gateT_sh", [DSH, 8])
    w_xprojT_sh = ein("w_xprojT_sh", [DSH, 96])
    w_dtT_sh = ein("w_dtT_sh", [64, DSH])
    b_dt_sh = ein("b_dt_sh", [DSH, 1])
    A0_sh = ein("A0_sh", [DSH, DN])
    D0_sh = ein("D0_sh", [DSH, 1])
    w1_e = ein("w1_e", [D, D])
    w2_e = ein("w2_e", [D, D])
    w_inprojT_sh = ein("w_inprojT_sh", [D, 2 * DSH1])
    conv_sh = ein("conv_sh", [DSH1, 4])
    w_xprojT1_sh = ein("w_xprojT1_sh", [DSH1, 160])
    w_dtprojT1_sh = ein("w_dtprojT1_sh", [128, DSH1])
    b_dt1_sh = ein("b_dt1_sh", [DSH1, 1])
    A1_sh = ein("A1_sh", [DSH1, DN])
    D1_sh = ein("D1_sh", [DSH1, 1])
    w_outprojT_sh = ein("w_outprojT_sh", [DSH1, D])
    st_g = [ein(f"st_g{k}", [2, 16, 128, 512]) for k in range(4)]
    st_u = [ein(f"st_u{k}", [2, 16, 128, 512]) for k in range(4)]
    st_d = [ein(f"st_d{k}", [2, 4, 128, 2048]) for k in range(4)]
    cf_g = [ein(f"cf_g{k}", [2, 16, 128, 512], BF16) for k in range(4)]
    cf_d = [ein(f"cf_d{k}", [2, 4, 128, 2048], BF16) for k in range(4)]
    ident = ein("ident", [128, 128], BF16)
    identf = ein("identf", [128, 128])
    seln = ein("seln", [16, 16 * 128], BF16)
    tokidf = ein("tokidf", [128, 64])
    iota1280 = ein("iota1280", [128, CAP])
    myeoh = ein("myeoh", [128, 8])

    out_sh = nc.dram_tensor("out_sh", [DSH, T], F32, kind="ExternalOutput").ap()

    cc_st_in = dint("cc_st_in", [2, T])
    cc_st_out = dint("cc_st_out", [2, T])
    cc_lg_in = dint("cc_lg_in", [8, T])
    cc_lg_out = dint("cc_lg_out", [8, T])
    ag_xn_in = dint("ag_xn_in", [DSH, T], BF16)
    xn_all = dint("xn_all", [D, T], BF16)
    ag_xnT_in = dint("ag_xnT_in", [T, DSH], BF16)
    xnT8_all = dint("xnT8_all", [NC * T, DSH], BF16)
    xdbl0_in = dint("xdbl0_in", [96, T])
    xdbl0_out = dint("xdbl0_out", [96, T])
    xdbl0b = dint("xdbl0b", [96, T], BF16)
    xnT_full = dint("xnT_full", [T, D], BF16)
    g_loc = dint("g_loc", [T, D], BF16)
    g_blk = dint("g_blk", [NC * T, DSH], BF16)
    g_rs = dint("g_rs", [T, DSH], BF16)
    w_dram = dint("w_dram", [1, T])
    inv_dram = dint("inv_dram", [1, CAP])
    s0_dram = dint("s0_dram", [DSH, T], BF16)
    x1_dram = dint("x1_dram", [DSH, T])
    xdbl1_in = dint("xdbl1_in", [160, T])
    xdbl1_out = dint("xdbl1_out", [160, T])
    xdbl1b = dint("xdbl1b", [160, T], BF16)
    x1p_dram = dint("x1p_dram", [DSH1, T])
    z_dram = dint("z_dram", [DSH1, T], BF16)
    xc_dram = dint("xc_dram", [DSH1, T], BF16)
    rs_in = dint("rs_in", [2 * D, T])
    rs_out = dint("rs_out", [2 * DSH, T])

    tap_aps = {}

    def tap(name, shape, dt=F32):
        if name in taps:
            tap_aps[name] = nc.dram_tensor("tap_" + name, shape, dt,
                                           kind="ExternalOutput").ap()
            return tap_aps[name]
        return None

    s_dy = b.sem("s_dy")
    s_dg = b.sem("s_dg")
    s_cc = b.sem("s_cc")
    s_v = b.sem("s_v")
    s_sc = b.sem("s_sc")
    s_t = b.sem("s_t")

    def alloc(name, shape, dt):
        return nc.alloc_sbuf_tensor(name, shape, dt)

    A1f = alloc("A1f", [128, 8200], F32)
    A2b = alloc("A2b", [128, 11392], BF16)
    A3b = alloc("A3b", [128, 24576], BF16)
    Hb = alloc("Hb", [128, DN, TC], BF16)
    dBub = alloc("dBub", [128, DN, TC], BF16)
    Bbc = alloc("Bbc", [128, DN, TC], BF16)
    Cbc = alloc("Cbc", [128, DN, TC], BF16)
    delta = alloc("delta", [128, TC], F32)
    spt1 = alloc("spt1", [128, TC], F32)
    spt2 = alloc("spt2", [128, TC], F32)
    uch = alloc("uch", [128, TC], BF16)
    duch = alloc("duch", [128, TC], BF16)
    zsil = alloc("zsil", [128, TC], BF16)
    ych = alloc("ych", [128, TC], F32)
    fix16 = alloc("fix16", [128, DN], F32)
    hprev = alloc("hprev", [128, 4, DN], F32)
    Acol = alloc("Acol", [128, 4, DN], F32)
    Dcol = alloc("Dcol", [128, 4], F32)
    bcol = alloc("bcol", [128, 4], F32)
    Bchb = alloc("Bchb", [16, TC], BF16)
    Cchb = alloc("Cchb", [16, TC], BF16)
    dlc = alloc("dlc", [128, TC], BF16)
    onesf = alloc("onesf", [128, 1], F32)
    c_eps = alloc("c_eps", [128, 1], F32)
    nc.const_aps.register(F32, EPS, c_eps[:, :])
    nc.const_aps.register(F32, 1.0, onesf[:, :])
    onesr = alloc("onesr", [1, 128], F32)
    xtile = alloc("xtile", [128, TC], F32)
    xftile = alloc("xftile", [128, TC], F32)
    xntile = alloc("xntile", [128, TC], BF16)
    identt = alloc("identt", [128, 128], BF16)
    identtf = alloc("identtf", [128, 128], F32)
    selnt = alloc("selnt", [16, 16 * 128], BF16)
    wxp0 = alloc("wxp0", [128, 2, 96], BF16)
    wdt0 = alloc("wdt0", [64, 2, 128], BF16)
    gws = alloc("gws", [128, 2, 8], F32)
    wv = alloc("wv", [128, 64], F32)
    invs = alloc("invs", [128, 10], I32)
    carry8 = alloc("carry8", [8, 1], F32)
    myeo = alloc("myeo", [128, 8], F32)
    convc = alloc("convc", [128, 4], F32)
    rpA = alloc("rpA", [128, TC], BF16)
    rpB = alloc("rpB", [128, TC], BF16)
    tapA = alloc("tapA", [128, 512], BF16)
    tapB = alloc("tapB", [128, 512], BF16)
    cof2 = alloc("cof2", [128, 2, 512], BF16)
    bz1 = alloc("bz1", [128, 512], BF16)
    bz2 = alloc("bz2", [128, 512], BF16)
    bz3 = alloc("bz3", [128, 512], BF16)
    sgbuf = alloc("sgbuf", [128, TC], BF16)
    A4b = alloc("A4b", [128, 2048], BF16)

    dA = A1f[:, 0:8192].rearrange("p (n t) -> p n t", n=DN)
    dAf = A1f[:, 0:8192]
    bigA = A1f[:, 0:8200]
    logT = A1f[:, 0:512]
    onehot = A1f[:, 512:1024]
    exf = A1f[:, 1024:1536]
    exclT = A1f[:, 1536:2048]
    keptT = A1f[:, 2048:2560]
    lgch = A1f[0:8, 2560:3072]
    mech = A1f[0:8, 3072:3584]
    inclch = A1f[0:8, 3584:4096]
    exclch = A1f[0:8, 4096:4608]
    ones8 = A1f[0:8, 4608:5120]
    strow = A1f[0:1, 5120:6144]
    maxv = A1f[:, 7680:7744]
    sumex = A1f[:, 7744:7808]
    bigt = A1f[:, 7808:7872]
    kany = A1f[:, 7872:7936]
    slotv = A1f[:, 7936:8000]
    mrow = A1f[0:1, 6144:6656]
    vrow = A1f[0:1, 6656:7168]
    rrow = A1f[0:1, 7168:7680]
    wipv = A1f[:, 0:8192].bitcast(BF16).rearrange("p (k j) -> p k j", k=16)
    moe_rhs = A3b[:, 8192:12288].rearrange("p (k j) -> p k j", k=16)
    moe_hm = A3b[:, 12288:16384].rearrange("p (k j) -> p k j", k=16)
    moe_of = A3b[:, 16384:20480].rearrange("p (k j) -> p k j", k=16)
    woutv = A2b[:, 0:8192].rearrange("p (k j) -> p k j", k=4)
    wxp1v = A2b[:, 8192:8832].rearrange("p (k j) -> p k j", k=4)
    wdt1v = A2b[:, 8832:9344].rearrange("p (k j) -> p k j", k=4)
    inter = A2b[:, 9344:11392].rearrange("p (k j) -> p k j", k=4)
    sch = A4b[:, 0:512]
    gchb = A4b[:, 512:1536].rearrange("p (a b) -> p a b", a=2)
    y3st = A4b[:, 0:2048].rearrange("p (a b) -> p a b", a=4)
    gtile = A3b[:, 0:2048]
    wAv = A3b[:, 2048:4096]
    wBv = A3b[:, 4096:6144]
    tmaj = A3b[:, 6144:8192]
    gshv = A3b[:, 0:8192].rearrange("p (k j) -> p k j", k=16)
    ushv = A3b[:, 8192:16384].rearrange("p (k j) -> p k j", k=16)
    dshv = A3b[:, 16384:24576].rearrange("p (k j) -> p k j", k=4)

    pm = nc.alloc_psum_tensor("pm", [128, 8, 256], F32)
    pb1 = nc.alloc_psum_tensor("pb1", [128, 512], F32)
    pb2 = nc.alloc_psum_tensor("pb2", [128, 512], F32)
    pg = nc.alloc_psum_tensor("pg", [128, 512], F32)
    ptr = pm[:, 4:6, :].rearrange("p a b -> p (a b)").bitcast(BF16)[:, 0:128]
    ptr2 = pm[:, 6:8, :].rearrange("p a b -> p (a b)").bitcast(BF16)[:, 0:128]

    v, s, t, g, y = b.v, b.s, b.t, b.g, b.y

    y.op("dma_start", out=identt[:, :], in_=ident, inc=(s_dy, 16))
    y.op("dma_start", out=selnt[:, :], in_=seln, inc=(s_dy, 16))
    y.op("dma_start", out=myeo[:, :], in_=myeoh, inc=(s_dy, 16))
    y.op("dma_start", out=identtf[:, :], in_=identf, inc=(s_dy, 16))
    for i in range(NTILE):
        y.op("dma_start", out=Acol[:, i, :], in_=A0_sh[i * 128:(i + 1) * 128, :], inc=(s_dy, 16))
        y.op("dma_start", out=Dcol[:, i:i + 1], in_=D0_sh[i * 128:(i + 1) * 128, :], inc=(s_dy, 16))
        y.op("dma_start", out=bcol[:, i:i + 1], in_=b_dt_sh[i * 128:(i + 1) * 128, :], inc=(s_dy, 16))
        y.op("dma_start", out=gws[:, i, :], in_=w_gateT_sh[i * 128:(i + 1) * 128, :], inc=(s_dy, 16))
    ld_const = s_dy.n
    for i in range(NTILE):
        g.op("dma_start", out=wxp0[:, i, :], in_=w_xprojT_sh[i * 128:(i + 1) * 128, :], inc=(s_dg, 16))
        g.op("dma_start", out=wdt0[:, i, :], in_=w_dtT_sh[:, i * 128:(i + 1) * 128], inc=(s_dg, 16))
    ld_wx = s_dg.n

    v.op("memset", onesf[:, :], 1.0)
    v.op("memset", c_eps[:, :], EPS)
    v.op("memset", onesr[:, :], 1.0)
    v.op("memset", hprev[:, :, :], 0.0)
    v.op("memset", carry8[:, :], 0.0)
    v.op("memset", ones8, 1.0)
    v.op("memset", bigt[:, :], BIGS, inc=(s_v, 1))
    s.wait(s_dy, ld_const)
    for i in range(NTILE):
        s.op("activation", out=Acol[:, i, :], in_=Acol[:, i, :], func=AF.Exp,
             inc=(s_sc, 1) if i == NTILE - 1 else None)
    v.wait(s_sc, s_sc.n)
    for i in range(NTILE):
        v.op("tensor_scalar_mul", out=Acol[:, i, :], in0=Acol[:, i, :], scalar1=-1.0,
             inc=(s_v, 1) if i == NTILE - 1 else None)
    v_init_done = s_v.n
    s.wait(s_v, v_init_done)
    t.wait(s_v, v_init_done)
    g.wait(s_v, v_init_done)
    y.wait(s_v, v_init_done)

    for k in range(NCH):
        for i in range(NTILE):
            y.wait(s_t, s_t.n)
            y.op("dma_start", out=xtile[:, :], in_=x_sh[i * 128:(i + 1) * 128, k * TC:(k + 1) * TC], inc=(s_dy, 16))
            s.wait(s_dy, s_dy.n)
            s.wait(s_t, s_t.n)
            s.op("activation", out=spt1[:, :], in_=xtile[:, :], func=AF.Square, inc=(s_sc, 1))
            t.wait(s_dy, s_dy.n)
            if k > 0 and i == 0:
                t.wait(s_v, s_v.n)
            t.op("matmul", pg[0:1, :], onesf[:, 0:1], xtile[:, :],
                 start=(i == 0), stop=(i == NTILE - 1), inc=(s_t, 1))
            t.wait(s_sc, s_sc.n)
            t.op("matmul", pb1[0:1, :], onesf[:, 0:1], spt1[:, :],
                 start=(i == 0), stop=(i == NTILE - 1), inc=(s_t, 1))
        v.wait(s_t, s_t.n)
        v.wait(s_dy, s_dy.n)
        v.op("tensor_copy", out=strow[0:1, 0:TC], in_=pg[0:1, :])
        v.op("tensor_copy", out=strow[0:1, TC:2 * TC], in_=pb1[0:1, :], inc=(s_v, 1))
        y.wait(s_v, s_v.n)
        y.op("dma_start", out=cc_st_in[0:1, k * TC:(k + 1) * TC], in_=strow[0:1, 0:TC], inc=(s_dy, 16))
        y.op("dma_start", out=cc_st_in[1:2, k * TC:(k + 1) * TC], in_=strow[0:1, TC:2 * TC], inc=(s_dy, 16))
    g.wait(s_dy, s_dy.n)
    g.op("collective_compute", "AllReduce", ALU.add,
         replica_groups=[list(range(NC))], ins=[cc_st_in.opt()],
         outs=[cc_st_out.opt()], inc=(s_cc, 1))
    cc_stats0 = s_cc.n

    y.wait(s_cc, cc_stats0)
    xdbl_view = pm[0:96, 0:2, :].rearrange("p a b -> p (a b)")
    for k in range(NCH):
        y.wait(s_v, s_v.n)
        y.op("dma_start", out=strow[0:1, 0:TC], in_=cc_st_out[0:1, k * TC:(k + 1) * TC], inc=(s_dy, 16))
        y.op("dma_start", out=strow[0:1, TC:2 * TC], in_=cc_st_out[1:2, k * TC:(k + 1) * TC], inc=(s_dy, 16))
        v.wait(s_dy, s_dy.n)
        v.wait(s_sc, s_sc.n)
        v.op("tensor_scalar_mul", out=mrow[0:1, :], in0=strow[0:1, 0:TC], scalar1=1.0 / D)
        v.op("tensor_tensor", out=rrow[0:1, :], in0=mrow[0:1, :], in1=mrow[0:1, :], op=ALU.mult)
        v.op("scalar_tensor_tensor", out=vrow[0:1, :], in0=strow[0:1, TC:2 * TC],
             scalar=1.0 / D, in1=rrow[0:1, :], op0=ALU.mult, op1=ALU.subtract, inc=(s_v, 1))
        s.wait(s_v, s_v.n)
        s.op("activation", out=rrow[0:1, :], in_=vrow[0:1, :], func=AF.Ln, bias=EPS)
        s.op("activation", out=vrow[0:1, :], in_=rrow[0:1, :], func=AF.Exp, scale=-0.5, inc=(s_sc, 1))
        t.wait(s_v, s_v.n)
        t.op("matmul", pb2[:, :], onesr[0:1, :], mrow[0:1, :], start=True, stop=True)
        t.wait(s_sc, s_sc.n)
        t.op("matmul", pg[:, :], onesr[0:1, :], vrow[0:1, :], start=True, stop=True, inc=(s_t, 1))
        bcast_t = s_t.n
        for i in range(NTILE):
            y.wait(s_t, s_t.n)
            y.op("dma_start", out=xtile[:, :], in_=x_sh[i * 128:(i + 1) * 128, k * TC:(k + 1) * TC], inc=(s_dy, 16))
            v.wait(s_dy, s_dy.n)
            v.wait(s_t, s_t.n)
            v.op("tensor_tensor", out=xftile[:, :], in0=xtile[:, :], in1=pb2[:, :], op=ALU.subtract)
            v.op("tensor_tensor", out=spt1[:, :], in0=xftile[:, :], in1=pg[:, :], op=ALU.mult)
            v.op("tensor_copy", out=xntile[:, :], in_=spt1[:, :])
            v.op("tensor_copy", out=xftile[:, :], in_=spt1[:, :], inc=(s_v, 1))
            xn_ready = s_v.n
            y.wait(s_v, xn_ready)
            y.op("dma_start", out=ag_xn_in[i * 128:(i + 1) * 128, k * TC:(k + 1) * TC], in_=xntile[:, :], inc=(s_dy, 16))
            t.wait(s_v, xn_ready)
            t.op("matmul", pb1[0:8, :], gws[:, i, :], xftile[:, :],
                 start=(i == 0), stop=(i == NTILE - 1))
            t.wait(s_dg, ld_wx)
            t.op("matmul", xdbl_view, wxp0[:, i, :], xntile[:, :],
                 start=(i == 0), stop=(i == NTILE - 1), inc=(s_t, 1))
            for j in range(4):
                pt = ptr[:, :] if j % 2 == 0 else ptr2[:, :]
                t.wait(s_sc, s_sc.n)
                t.op("transpose", pt, xntile[:, j * 128:(j + 1) * 128], identt[:, :], inc=(s_t, 1))
                s.wait(s_t, s_t.n)
                s.wait(s_dy, s_dy.n)
                s.op("copy", out=tmaj[:, (i * 4 + j) * 128:(i * 4 + j + 1) * 128], in_=pt, inc=(s_sc, 1))
                y.wait(s_sc, s_sc.n)
                y.op("dma_start", out=ag_xnT_in[k * TC + j * 128:k * TC + (j + 1) * 128,
                                               i * 128:(i + 1) * 128],
                     in_=tmaj[:, (i * 4 + j) * 128:(i * 4 + j + 1) * 128], inc=(s_dy, 16))
        s.wait(s_t, s_t.n)
        s.wait(s_dy, s_dy.n)
        s.op("copy", out=lgch, in_=pb1[0:8, :], inc=(s_sc, 1))
        s.op("copy", out=spt2[0:96, :], in_=xdbl_view, inc=(s_sc, 1))
        y.wait(s_sc, s_sc.n)
        y.op("dma_start", out=cc_lg_in[:, k * TC:(k + 1) * TC], in_=lgch, inc=(s_dy, 16))
        y.op("dma_start", out=xdbl0_in[:, k * TC:(k + 1) * TC], in_=spt2[0:96, :], inc=(s_dy, 16))
    g.wait(s_dy, s_dy.n)
    g.op("collective_compute", "AllReduce", ALU.add,
         replica_groups=[list(range(NC))], ins=[cc_lg_in.opt()],
         outs=[cc_lg_out.opt()], inc=(s_cc, 1))
    cc_lg = s_cc.n
    g.op("collective_compute", "AllReduce", ALU.add,
         replica_groups=[list(range(NC))], ins=[xdbl0_in.opt()],
         outs=[xdbl0_out.opt()], inc=(s_cc, 1))
    cc_xdbl0 = s_cc.n
    g.op("collective_compute", "AllGather", ALU.bypass,
         replica_groups=[list(range(NC))], ins=[ag_xn_in.opt()],
         outs=[xn_all.opt()], inc=(s_cc, 1))
    cc_xn = s_cc.n
    g.op("collective_compute", "AllGather", ALU.bypass,
         replica_groups=[list(range(NC))], ins=[ag_xnT_in.opt()],
         outs=[xnT8_all.opt()], inc=(s_cc, 1))
    cc_xnT = s_cc.n

    g.wait(s_cc, cc_xdbl0)
    for k in range(NCH):
        g.op("dma_start", out=spt1[0:96, :].bitcast(BF16)[:, 0:TC],
             in_=xdbl0_out[:, k * TC:(k + 1) * TC], inc=(s_dg, 16))
        g.wait(s_dg, s_dg.n)
        g.op("dma_start", out=xdbl0b[:, k * TC:(k + 1) * TC],
             in_=spt1[0:96, :].bitcast(BF16)[:, 0:TC], inc=(s_dg, 16))
    dg_xdbl0b = s_dg.n

    tp = tap("xn_all", [D, T], BF16)
    if tp is not None:
        y.wait(s_cc, cc_xn)
        for r in range(16):
            y.op("dma_start", out=tp[r * 128:(r + 1) * 128, :],
                 in_=xn_all[r * 128:(r + 1) * 128, :], inc=(s_dy, 16))
    tp = tap("xdbl0", [96, T])
    if tp is not None:
        y.wait(s_cc, cc_xdbl0)
        y.op("dma_start", out=tp, in_=xdbl0_out, inc=(s_dy, 16))
    tp = tap("logits", [8, T])
    if tp is not None:
        y.wait(s_cc, cc_lg)
        y.op("dma_start", out=tp, in_=cc_lg_out, inc=(s_dy, 16))
    tp = tap("xnT8", [NC * T, DSH], BF16)
    if tp is not None:
        y.wait(s_cc, cc_xnT)
        for r in range(NC):
            y.op("dma_start", out=tp[r * T:(r + 1) * T, :],
                 in_=xnT8_all[r * T:(r + 1) * T, :], inc=(s_dy, 16))

    def finish():
        y.wait(s_dy, s_dy.n)
        y.wait(s_dg, s_dg.n)
        g.wait(s_dg, s_dg.n)
        return _finalize(nc, b), tap_aps

    if upto <= 1:
        return finish()

    onehot3 = onehot.rearrange("p (a b) -> p a b", b=8)
    logT3 = logT.rearrange("p (a b) -> p a b", b=8)
    exf3 = exf.rearrange("p (a b) -> p a b", b=8)
    keptT3 = keptT.rearrange("p (a b) -> p a b", b=8)
    exclT3 = exclT.rearrange("p (a b) -> p a b", b=8)
    y.wait(s_cc, cc_lg)
    for k in range(NCH):
        y.wait(s_sc, s_sc.n)
        y.op("dma_start", out=lgch, in_=cc_lg_out[:, k * TC:(k + 1) * TC], inc=(s_dy, 16))
        for j in range(4):
            pt = pb1[:, 0:128] if j % 2 == 0 else pb2[:, 0:128]
            t.wait(s_dy, s_dy.n)
            t.wait(s_sc, s_sc.n)
            t.op("transpose", pt[:, 0:8], lgch[0:8, j * 128:(j + 1) * 128], identtf[0:8, 0:8], inc=(s_t, 1))
            s.wait(s_t, s_t.n)
            s.op("copy", out=logT[:, (4 * k + j) * 8:(4 * k + j + 1) * 8], in_=pt[:, 0:8], inc=(s_sc, 1))
    v.wait(s_sc, s_sc.n)
    v.op("tensor_reduce", out=maxv[:, :].rearrange("p (a b) -> p a b", b=1), in_=logT3,
         axis=mybir.AxisListType.X, op=ALU.max)
    mxb = maxv[:, :].unsqueeze(2).broadcast_to([128, 64, 8])
    v.op("tensor_tensor", out=onehot3, in0=logT3, in1=mxb, op=ALU.is_ge)
    v.op("tensor_tensor", out=exf3, in0=logT3, in1=mxb, op=ALU.subtract, inc=(s_v, 1))
    s.wait(s_v, s_v.n)
    s.op("activation", out=exclT, in_=exf, func=AF.Exp, inc=(s_sc, 1))
    v.wait(s_sc, s_sc.n)
    v.op("tensor_reduce", out=sumex[:, :].rearrange("p (a b) -> p a b", b=1), in_=exclT3,
         axis=mybir.AxisListType.X, op=ALU.add)
    v.op("reciprocal", out=wv[:, :], in_=sumex[:, :], inc=(s_v, 1))
    for k in range(NCH):
        for j in range(4):
            pt = pb1[:, 0:128] if j % 2 == 0 else pb2[:, 0:128]
            t.wait(s_v, s_v.n)
            t.wait(s_sc, s_sc.n)
            t.op("transpose", pt[0:8, :], onehot[:, (4 * k + j) * 8:(4 * k + j + 1) * 8], identtf[:, :], inc=(s_t, 1))
            s.wait(s_t, s_t.n)
            s.op("copy", out=mech[:, j * 128:(j + 1) * 128], in_=pt[0:8, :], inc=(s_sc, 1))
        v.wait(s_sc, s_sc.n)
        v.wait(s_v, s_v.n)
        v.op("tensor_tensor_scan", out=inclch, data0=ones8, data1=mech,
             initial=carry8[:, 0:1], op0=ALU.mult, op1=ALU.add, inc=(s_v, 1))
        v.wait(s_v, s_v.n)
        v.op("tensor_copy", out=carry8[:, 0:1], in_=inclch[:, TC - 1:TC], inc=(s_v, 1))
        v.op("tensor_tensor", out=exclch, in0=inclch, in1=mech, op=ALU.subtract, inc=(s_v, 1))
        t.wait(s_v, s_v.n)
        t.wait(s_sc, s_sc.n)
        for j in range(4):
            t.op("transpose", pg[:, j * 8:(j + 1) * 8], exclch[0:8, j * 128:(j + 1) * 128],
                 identtf[0:8, 0:8], inc=(s_t, 1) if j == 3 else None)
        s.wait(s_t, s_t.n)
        s.op("copy", out=exclT[:, (4 * k) * 8:(4 * k + 4) * 8], in_=pg[:, 0:32], inc=(s_sc, 1))
    v.wait(s_sc, s_sc.n)
    v.op("tensor_single_scalar", out=exf, in_=exclT, scalar=float(CAP), op=ALU.is_lt)
    v.op("tensor_tensor", out=keptT, in0=exf, in1=onehot, op=ALU.mult)
    myb = myeo[:, :].unsqueeze(1).broadcast_to([128, 64, 8])
    v.op("tensor_tensor", out=exf3, in0=exclT3, in1=myb, op=ALU.mult)
    v.op("tensor_reduce", out=maxv[:, :].rearrange("p (a b) -> p a b", b=1), in_=exf3,
         axis=mybir.AxisListType.X, op=ALU.add)
    v.op("tensor_tensor", out=exf3, in0=keptT3, in1=myb, op=ALU.mult)
    v.op("tensor_reduce", out=slotv[:, :].rearrange("p (a b) -> p a b", b=1), in_=exf3,
         axis=mybir.AxisListType.X, op=ALU.add)
    v.op("tensor_tensor", out=bigt[:, :], in0=maxv[:, :], in1=slotv[:, :], op=ALU.mult)
    v.op("tensor_scalar", out=maxv[:, :], in0=slotv[:, :], scalar1=-BIGS, scalar2=BIGS,
         op0=ALU.mult, op1=ALU.add)
    v.op("tensor_tensor", out=kany[:, :], in0=bigt[:, :], in1=maxv[:, :], op=ALU.add,
         inc=(s_v, 1))
    idx_done = s_v.n
    iotav = A1f[:, 0:CAP]
    ohv = A1f[:, CAP:2 * CAP]
    tokf = A1f[:, 2 * CAP:2 * CAP + 64]
    y.wait(s_v, idx_done)
    y.op("dma_start", out=iotav, in_=iota1280, inc=(s_dy, 16))
    y.op("dma_start", out=tokf, in_=tokidf, inc=(s_dy, 16))
    inv_segs = [(pg[0:1, :], 0, 512), (pb1[0:1, :], 512, 512), (pb2[0:1, :], 1024, 256)]
    occ_segs = [(pm[0:1, 0:2, :].rearrange("p a b -> p (a b)"), 0, 512),
                (pm[0:1, 2:4, :].rearrange("p a b -> p (a b)"), 512, 512),
                (pm[0:1, 4:6, :].rearrange("p a b -> p (a b)"), 1024, 256)]
    t.wait(s_sc, s_sc.n)
    t.wait(s_dy, s_dy.n)
    tp_oh = tap("oh0", [128, CAP])
    tp_kany = tap("kany", [128, 64])
    for grp in range(64):
        v.wait(s_dy, s_dy.n)
        if grp > 0:
            v.wait(s_t, s_t.n)
        v.op("tensor_single_scalar", out=ohv, in_=iotav,
             scalar=kany[:, grp:grp + 1], op=ALU.is_equal, inc=(s_v, 1))
        if grp == 0 and tp_oh is not None:
            y.wait(s_v, s_v.n)
            y.op("dma_start", out=tp_oh, in_=ohv, inc=(s_dy, 16))
            y.op("dma_start", out=tp_kany, in_=kany[:, :], inc=(s_dy, 16))
            t.wait(s_dy, s_dy.n)
        t.wait(s_v, s_v.n)
        for (prow, off, ln) in inv_segs:
            t.op("matmul", prow[:, 0:ln], tokf[:, grp:grp + 1], ohv[:, off:off + ln],
                 start=(grp == 0), stop=(grp == 63))
        for qi, (prow, off, ln) in enumerate(occ_segs):
            t.op("matmul", prow[:, 0:ln], onesf[:, 0:1], ohv[:, off:off + ln],
                 start=(grp == 0), stop=(grp == 63),
                 inc=(s_t, 1) if qi == 2 else None)
    inv_rowS = A1f[0:1, 2560:2560 + CAP]
    occ_rowS = A1f[0:1, 3840:3840 + CAP]
    t2row = A1f[0:1, 5120:5120 + CAP]
    invf_row = A1f[0:1, 6400:6400 + CAP]
    s.wait(s_t, s_t.n)
    for (prow, off, ln) in inv_segs:
        s.op("copy", out=inv_rowS[:, off:off + ln], in_=prow[:, 0:ln])
    for qi, (prow, off, ln) in enumerate(occ_segs):
        s.op("copy", out=occ_rowS[:, off:off + ln], in_=prow[:, 0:ln],
             inc=(s_sc, 1) if qi == 2 else None)
    v.wait(s_sc, s_sc.n)
    v.op("tensor_scalar", out=t2row, in0=occ_rowS, scalar1=-BIGS, scalar2=BIGS,
         op0=ALU.mult, op1=ALU.add)
    v.op("tensor_tensor", out=invf_row, in0=inv_rowS, in1=t2row, op=ALU.add, inc=(s_v, 1))
    y.wait(s_v, s_v.n)
    y.op("dma_start", out=inv_dram, in_=invf_row, inc=(s_dy, 16))
    g.wait(s_dy, s_dy.n)
    g.op("dma_start", out=maxv[:, 0:10], in_=inv_dram.rearrange("o (q p) -> p (q o)", p=128),
         inc=(s_dg, 16), allow_slow_non_contiguous=True)
    inv_loaded = s_dg.n
    v.wait(s_dg, inv_loaded)
    v.op("tensor_copy", out=invs[:, :], in_=maxv[:, 0:10], inc=(s_v, 1))
    t.wait(s_v, s_v.n)
    t.op("transpose", pb1[0:64, 0:128], wv[:, :], identtf[:, :], inc=(s_t, 1))
    wrowS = A1f[0:64, 2560:2688]
    s.wait(s_t, s_t.n)
    s.op("copy", out=wrowS, in_=pb1[0:64, 0:128], inc=(s_sc, 1))
    y.wait(s_sc, s_sc.n)
    y.op("dma_start", out=w_dram.rearrange("o (a b) -> (o a) b", b=128), in_=wrowS, inc=(s_dy, 16))
    rt_done = s_v.n
    g.wait(s_cc, cc_xnT)
    for cb in range(NC):
        for seg in range(8):
            g.op("dma_start",
                 out=xnT_full[seg * 1024:(seg + 1) * 1024, cb * DSH:(cb + 1) * DSH],
                 in_=xnT8_all[cb * T + seg * 1024:cb * T + (seg + 1) * 1024, :],
                 inc=(s_dg, 16))
    xnTf_done = s_dg.n

    tp = tap("invd", [1, CAP])
    if tp is not None:
        y.wait(s_dg, inv_loaded)
        y.op("dma_start", out=tp, in_=inv_dram, inc=(s_dy, 16))
    tp = tap("occd", [1, CAP])
    if tp is not None:
        y.wait(s_dg, inv_loaded)
        y.op("dma_start", out=tp, in_=occ_rowS, inc=(s_dy, 16))
    tp = tap("invraw", [1, CAP])
    if tp is not None:
        y.wait(s_dg, inv_loaded)
        y.op("dma_start", out=tp, in_=inv_rowS, inc=(s_dy, 16))
    tp = tap("kanyf", [128, 64])
    if tp is not None:
        y.wait(s_dg, inv_loaded)
        y.op("dma_start", out=tp, in_=kany[:, :], inc=(s_dy, 16))
    tp = tap("inv", [128, 10], I32)
    if tp is not None:
        v.wait(s_v, rt_done)
        y.wait(s_v, rt_done)
        y.op("dma_start", out=tp, in_=invs[:, :], inc=(s_dy, 16))
    tp = tap("wvt", [128, 64])
    if tp is not None:
        y.wait(s_v, rt_done)
        y.op("dma_start", out=tp, in_=wv[:, :], inc=(s_dy, 16))
    if upto <= 2:
        return finish()

    NG = CAP // 256

    def moe_group(grp):
        for sub in range(2):
            g.wait(s_t, s_t.n)
            g.wait(s_dg, xnTf_done)
            g.op("indirect_dma_start", out=gtile[:, :],
                 out_offset=None, in_=xnT_full,
                 in_offset=IndirectOffsetOnAxis(invs[:, 2 * grp + sub:2 * grp + sub + 1], 0),
                 bounds_check=T - 1, oob_is_err=False, inc=(s_dg, 16))
            for hb in range(16):
                pt = ptr[:, :] if hb % 2 == 0 else ptr2[:, :]
                t.wait(s_dg, s_dg.n)
                t.wait(s_sc, s_sc.n)
                t.op("transpose", pt, gtile[:, hb * 128:(hb + 1) * 128], identt[:, :], inc=(s_t, 1))
                s.wait(s_t, s_t.n)
                s.op("copy", out=moe_rhs[:, hb, sub * 128:(sub + 1) * 128], in_=pt, inc=(s_sc, 1))
        if grp == 0:
            tpr = tap("mrhs", [128, 16, 256], BF16)
            if tpr is not None:
                y.wait(s_sc, s_sc.n)
                y.op("dma_start", out=tpr.rearrange("p a b -> p (a b)"),
                     in_=moe_rhs[:, :, :].rearrange("p a b -> p (a b)"), inc=(s_dy, 16))
                t.wait(s_dy, s_dy.n)
        for ihalf in range(2):
            for kk in range(16):
                wv_ = wAv[:, 0:1024] if kk % 2 == 0 else wBv[:, 0:1024]
                g.wait(s_t, s_t.n)
                g.op("dma_start", out=wv_, in_=w1_e[kk * 128:(kk + 1) * 128,
                                                    ihalf * 1024:(ihalf + 1) * 1024], inc=(s_dg, 16))
                t.wait(s_dg, s_dg.n)
                if kk == 0:
                    t.wait(s_sc, s_sc.n)
                for ii in range(8):
                    t.op("matmul", pm[:, ii, :], wv_[:, ii * 128:(ii + 1) * 128],
                         moe_rhs[:, kk, :], start=(kk == 0 and ii % 2 == 0),
                         stop=(kk == 15), inc=(s_t, 1) if ii == 7 else None)
            s.wait(s_t, s_t.n)
            for ii in range(8):
                s.op("activation", out=moe_hm[:, ihalf * 8 + ii, :], in_=pm[:, ii, :],
                     func=AF.Gelu, inc=(s_sc, 1))
        if grp == 0:
            tph = tap("mhm", [128, 16, 256], BF16)
            if tph is not None:
                y.wait(s_sc, s_sc.n)
                y.op("dma_start", out=tph.rearrange("p a b -> p (a b)"),
                     in_=moe_hm[:, :, :].rearrange("p a b -> p (a b)"), inc=(s_dy, 16))
                t.wait(s_dy, s_dy.n)
        for hhalf in range(2):
            for kk in range(16):
                wv_ = wAv[:, 0:1024] if kk % 2 == 0 else wBv[:, 0:1024]
                g.wait(s_t, s_t.n)
                g.op("dma_start", out=wv_, in_=w2_e[kk * 128:(kk + 1) * 128,
                                                    hhalf * 1024:(hhalf + 1) * 1024], inc=(s_dg, 16))
                t.wait(s_dg, s_dg.n)
                if kk == 0:
                    t.wait(s_sc, s_sc.n)
                for hh in range(8):
                    t.op("matmul", pm[:, hh, :], wv_[:, hh * 128:(hh + 1) * 128],
                         moe_hm[:, kk, :], start=(kk == 0 and hh % 2 == 0),
                         stop=(kk == 15), inc=(s_t, 1) if hh == 7 else None)
            s.wait(s_t, s_t.n)
            for hh in range(8):
                s.op("copy", out=moe_of[:, hhalf * 8 + hh, :], in_=pm[:, hh, :], inc=(s_sc, 1))
        for sub in range(2):
            for hb in range(16):
                pt = ptr[:, :] if hb % 2 == 0 else ptr2[:, :]
                t.wait(s_sc, s_sc.n)
                t.wait(s_dg, s_dg.n)
                t.op("transpose", pt, moe_of[:, hb, sub * 128:(sub + 1) * 128], identt[:, :], inc=(s_t, 1))
                s.wait(s_t, s_t.n)
                s.op("copy", out=tmaj[:, hb * 128:(hb + 1) * 128], in_=pt, inc=(s_sc, 1))
            g.wait(s_sc, s_sc.n)
            g.wait(s_dy, gz_done)
            g.op("indirect_dma_start", out=g_loc,
                 out_offset=IndirectOffsetOnAxis(invs[:, 2 * grp + sub:2 * grp + sub + 1], 0),
                 in_=tmaj[:, :], in_offset=None,
                 bounds_check=T - 1, oob_is_err=False, inc=(s_dg, 16))

    dlc64 = dlc[0:64, :]

    def bcast_build(k, Bsrc, Csrc):
        y.wait(s_t, s_t.n)
        y.op("dma_start", out=Bchb[:, :], in_=Bsrc[:, k * TC:(k + 1) * TC], inc=(s_dy, 16))
        y.op("dma_start", out=Cchb[:, :], in_=Csrc[:, k * TC:(k + 1) * TC], inc=(s_dy, 16))
        t.wait(s_dy, s_dy.n)
        for n in range(DN):
            pt = pb1 if n % 2 == 0 else pb2
            t.wait(s_sc, s_sc.n)
            t.wait(s_v, s_v.n)
            t.op("matmul", pt[:, :], selnt[0:16, n * 128:(n + 1) * 128], Bchb[:, :],
                 start=True, stop=True, inc=(s_t, 1))
            s.wait(s_t, s_t.n)
            s.op("copy", out=Bbc[:, n, :], in_=pt[:, :], inc=(s_sc, 1))
        for n in range(DN):
            pt = pb1 if n % 2 == 0 else pb2
            t.wait(s_sc, s_sc.n)
            t.op("matmul", pt[:, :], selnt[0:16, n * 128:(n + 1) * 128], Cchb[:, :],
                 start=True, stop=True, inc=(s_t, 1))
            s.wait(s_t, s_t.n)
            s.op("copy", out=Cbc[:, n, :], in_=pt[:, :], inc=(s_sc, 1))

    Hflat = Hb[:, :, :].rearrange("p n t -> p (n t)")
    dBuflat = dBub[:, :, :].rearrange("p n t -> p (n t)")
    Cbcflat = Cbc[:, :, :].rearrange("p n t -> p (n t)")
    Bbcflat = Bbc[:, :, :].rearrange("p n t -> p (n t)")

    def l0_scan_chunk(k):
        y.wait(s_t, s_t.n)
        y.op("dma_start", out=dlc64, in_=xdbl0b[0:64, k * TC:(k + 1) * TC], inc=(s_dy, 16))
        bcast_build(k, xdbl0b[64:80, :], xdbl0b[80:96, :])
        for i in range(NTILE):
            t.wait(s_dy, s_dy.n)
            t.wait(s_sc, s_sc.n)
            t.op("matmul", pg[:, :], wdt0[:, i, :], dlc64, start=True, stop=True, inc=(s_t, 1))
            s.wait(s_t, s_t.n)
            s.wait(s_v, s_v.n)
            s.op("activation", out=spt1[:, :], in_=pg[:, :], func=AF.Abs, bias=bcol[:, i:i + 1])
            s.op("activation", out=spt2[:, :], in_=spt1[:, :], func=AF.Exp, scale=-1.0)
            s.op("activation", out=spt1[:, :], in_=spt2[:, :], func=AF.Ln, bias=1.0)
            s.op("activation", out=spt2[:, :], in_=pg[:, :], func=AF.Relu, bias=bcol[:, i:i + 1], inc=(s_sc, 1))
            v.wait(s_sc, s_sc.n)
            v.op("tensor_tensor", out=delta[:, :], in0=spt1[:, :], in1=spt2[:, :], op=ALU.add, inc=(s_v, 1))
            delta_done = s_v.n
            y.wait(s_v, s_v.n)
            y.wait(s_dy, s_dy.n)
            y.op("dma_start", out=uch[:, :], in_=ag_xn_in[i * 128:(i + 1) * 128, k * TC:(k + 1) * TC], inc=(s_dy, 16))
            s.wait(s_v, delta_done)
            for n in range(DN):
                s.op("activation", out=dA[:, n, :], in_=delta[:, :], func=AF.Exp,
                     scale=Acol[:, i, n:n + 1], inc=(s_sc, 1) if n == DN - 1 else None)
            dA_done = s_sc.n
            v.wait(s_dy, s_dy.n)
            v.op("tensor_tensor", out=duch[:, :], in0=delta[:, :], in1=uch[:, :], op=ALU.mult)
            dub = duch[:, :].unsqueeze(1).broadcast_to([128, DN, TC])
            v.op("tensor_tensor", out=dBub[:, :, :], in0=dub, in1=Bbc[:, :, :], op=ALU.mult)
            v.wait(s_sc, dA_done)
            v.op("tensor_tensor", out=fix16[:, :], in0=dA[:, :, 0], in1=hprev[:, i, :], op=ALU.mult)
            v.op("tensor_tensor", out=dBub[:, :, 0], in0=dBub[:, :, 0], in1=fix16[:, :], op=ALU.add)
            v.op("memset", dA[:, :, 0], 0.0)
            v.op("tensor_tensor_scan", out=Hflat, data0=dAf, data1=dBuflat,
                 initial=0.0, op0=ALU.mult, op1=ALU.add, inc=(s_v, 1))
            v.wait(s_v, s_v.n)
            v.op("tensor_copy", out=hprev[:, i, :], in_=Hb[:, :, TC - 1])
            v.op("tensor_tensor", out=dBuflat, in0=Hflat, in1=Cbcflat, op=ALU.mult)
            v.op("tensor_tensor", out=Hb[:, 0:8, :], in0=dBub[:, 0:8, :], in1=dBub[:, 8:16, :], op=ALU.add)
            v.op("tensor_tensor", out=dBub[:, 0:4, :], in0=Hb[:, 0:4, :], in1=Hb[:, 4:8, :], op=ALU.add)
            v.op("tensor_tensor", out=Hb[:, 0:2, :], in0=dBub[:, 0:2, :], in1=dBub[:, 2:4, :], op=ALU.add)
            v.op("tensor_tensor", out=ych[:, :], in0=Hb[:, 0:1, :].rearrange("p a t -> p (a t)"),
                 in1=Hb[:, 1:2, :].rearrange("p a t -> p (a t)"), op=ALU.add)
            v.op("scalar_tensor_tensor", out=sch[:, :], in0=uch[:, :], scalar=Dcol[:, i:i + 1],
                 in1=ych[:, :], op0=ALU.mult, op1=ALU.add, inc=(s_v, 1))
            y.wait(s_v, s_v.n)
            y.op("dma_start", out=s0_dram[i * 128:(i + 1) * 128, k * TC:(k + 1) * TC], in_=sch[:, :], inc=(s_dy, 16))

    v.wait(s_v, rt_done)
    v.op("memset", gtile, 0.0, inc=(s_v, 1))
    y.wait(s_v, s_v.n)
    for seg in range(64):
        y.op("dma_start", out=g_loc[seg * 128:(seg + 1) * 128, :], in_=gtile[:, :], inc=(s_dy, 16))
    gz_done = s_dy.n
    s.wait(s_v, rt_done)
    v.wait(s_dg, dg_xdbl0b)
    g.wait(s_dy, gz_done)
    for k in range(NCH):
        if k < NG:
            moe_group(k)
        l0_scan_chunk(k)
    g.wait(s_dg, s_dg.n)
    for cb in range(NC):
        for seg in range(8):
            g.op("dma_start",
                 out=g_blk[cb * T + seg * 1024:cb * T + (seg + 1) * 1024, :],
                 in_=g_loc[seg * 1024:(seg + 1) * 1024, cb * DSH:(cb + 1) * DSH],
                 inc=(s_dg, 16))
    g.wait(s_dg, s_dg.n)
    g.op("collective_compute", "ReduceScatter", ALU.add,
         replica_groups=[list(range(NC))], ins=[g_blk.opt()],
         outs=[g_rs.opt()], inc=(s_cc, 1))
    cc_eout = s_cc.n

    tp = tap("s0", [DSH, T], BF16)
    if tp is not None:
        y.wait(s_dy, s_dy.n)
        y.op("dma_start", out=tp, in_=s0_dram, inc=(s_dy, 16))
    tp = tap("xnTf", [T, D], BF16)
    if tp is not None:
        y.wait(s_dg, xnTf_done)
        for r in range(8):
            y.op("dma_start", out=tp[r * 1024:(r + 1) * 1024, :],
                 in_=xnT_full[r * 1024:(r + 1) * 1024, :], inc=(s_dy, 16))
    tp = tap("grs", [T, DSH], BF16)
    if tp is not None:
        y.wait(s_cc, cc_eout)
        y.op("dma_start", out=tp, in_=g_rs, inc=(s_dy, 16))
    if upto <= 3:
        return finish()

    for k in range(NCH):
        for i in range(NTILE):
            y.wait(s_cc, cc_eout)
            y.wait(s_v, s_v.n)
            y.op("dma_start", out=gchb[:, i, :],
                 in_=g_rs[k * TC:(k + 1) * TC, i * 128:(i + 1) * 128],
                 transpose=True, inc=(s_dy, 16))
        y.op("dma_start", out=ych[:, :],
             in_=w_dram[0:1, k * TC:(k + 1) * TC].broadcast_to([128, TC]),
             inc=(s_dy, 16))
        for i in range(NTILE):
            y.wait(s_t, s_t.n)
            y.op("dma_start", out=xtile[:, :], in_=x_sh[i * 128:(i + 1) * 128, k * TC:(k + 1) * TC], inc=(s_dy, 16))
            y.wait(s_v, s_v.n)
            y.wait(s_dy, s_dy.n)
            y.op("dma_start", out=sch[:, :], in_=s0_dram[i * 128:(i + 1) * 128, k * TC:(k + 1) * TC], inc=(s_dy, 16))
            v.wait(s_dy, s_dy.n)
            v.wait(s_sc, s_sc.n)
            v.op("tensor_tensor", out=spt1[:, :], in0=gchb[:, i, :], in1=ych[:, :], op=ALU.mult, inc=(s_v, 1))
            s.wait(s_v, s_v.n)
            s.op("activation", out=zsil[:, :], in_=spt1[:, :], func=AF.Silu, inc=(s_sc, 1))
            v.wait(s_sc, s_sc.n)
            v.op("tensor_tensor", out=spt1[:, :], in0=sch[:, :], in1=zsil[:, :], op=ALU.mult)
            v.op("tensor_tensor", out=spt2[:, :], in0=xtile[:, :], in1=spt1[:, :], op=ALU.add, inc=(s_v, 1))
            x1_done = s_v.n
            y.wait(s_v, x1_done)
            y.op("dma_start", out=x1_dram[i * 128:(i + 1) * 128, k * TC:(k + 1) * TC], in_=spt2[:, :], inc=(s_dy, 16))
            s.wait(s_v, x1_done)
            s.op("activation", out=spt1[:, :], in_=spt2[:, :], func=AF.Square, inc=(s_sc, 1))
            t.wait(s_v, x1_done)
            if k > 0 and i == 0:
                t.wait(s_v, s_v.n)
            t.op("matmul", pg[0:1, :], onesf[:, 0:1], spt2[:, :],
                 start=(i == 0), stop=(i == NTILE - 1), inc=(s_t, 1))
            t.wait(s_sc, s_sc.n)
            t.op("matmul", pb1[0:1, :], onesf[:, 0:1], spt1[:, :],
                 start=(i == 0), stop=(i == NTILE - 1), inc=(s_t, 1))
        v.wait(s_t, s_t.n)
        v.wait(s_dy, s_dy.n)
        v.op("tensor_copy", out=strow[0:1, 0:TC], in_=pg[0:1, :])
        v.op("tensor_copy", out=strow[0:1, TC:2 * TC], in_=pb1[0:1, :], inc=(s_v, 1))
        y.wait(s_v, s_v.n)
        y.op("dma_start", out=cc_st_in[0:1, k * TC:(k + 1) * TC], in_=strow[0:1, 0:TC], inc=(s_dy, 16))
        y.op("dma_start", out=cc_st_in[1:2, k * TC:(k + 1) * TC], in_=strow[0:1, TC:2 * TC], inc=(s_dy, 16))
    g.wait(s_dy, s_dy.n)
    g.op("collective_compute", "AllReduce", ALU.add,
         replica_groups=[list(range(NC))], ins=[cc_st_in.opt()],
         outs=[cc_st_out.opt()], inc=(s_cc, 1))
    cc_stats1 = s_cc.n

    y.wait(s_cc, cc_stats1)
    for k in range(NCH):
        y.wait(s_v, s_v.n)
        y.op("dma_start", out=strow[0:1, 0:TC], in_=cc_st_out[0:1, k * TC:(k + 1) * TC], inc=(s_dy, 16))
        y.op("dma_start", out=strow[0:1, TC:2 * TC], in_=cc_st_out[1:2, k * TC:(k + 1) * TC], inc=(s_dy, 16))
        v.wait(s_dy, s_dy.n)
        v.wait(s_sc, s_sc.n)
        v.op("tensor_scalar_mul", out=mrow[0:1, :], in0=strow[0:1, 0:TC], scalar1=1.0 / D)
        v.op("tensor_tensor", out=rrow[0:1, :], in0=mrow[0:1, :], in1=mrow[0:1, :], op=ALU.mult)
        v.op("scalar_tensor_tensor", out=vrow[0:1, :], in0=strow[0:1, TC:2 * TC],
             scalar=1.0 / D, in1=rrow[0:1, :], op0=ALU.mult, op1=ALU.subtract, inc=(s_v, 1))
        s.wait(s_v, s_v.n)
        s.op("activation", out=rrow[0:1, :], in_=vrow[0:1, :], func=AF.Ln, bias=EPS)
        s.op("activation", out=vrow[0:1, :], in_=rrow[0:1, :], func=AF.Exp, scale=-0.5, inc=(s_sc, 1))
        t.wait(s_v, s_v.n)
        t.op("matmul", pb2[:, :], onesr[0:1, :], mrow[0:1, :], start=True, stop=True)
        t.wait(s_sc, s_sc.n)
        t.op("matmul", pg[:, :], onesr[0:1, :], vrow[0:1, :], start=True, stop=True, inc=(s_t, 1))
        for i in range(NTILE):
            y.wait(s_t, s_t.n)
            y.op("dma_start", out=xtile[:, :], in_=x1_dram[i * 128:(i + 1) * 128, k * TC:(k + 1) * TC], inc=(s_dy, 16))
            v.wait(s_dy, s_dy.n)
            v.wait(s_t, s_t.n)
            v.op("tensor_tensor", out=spt1[:, :], in0=xtile[:, :], in1=pb2[:, :], op=ALU.subtract)
            v.op("tensor_tensor", out=xntile[:, :], in0=spt1[:, :], in1=pg[:, :], op=ALU.mult, inc=(s_v, 1))
            y.wait(s_v, s_v.n)
            y.op("dma_start", out=ag_xn_in[i * 128:(i + 1) * 128, k * TC:(k + 1) * TC], in_=xntile[:, :], inc=(s_dy, 16))
    g.wait(s_dy, s_dy.n)
    g.op("collective_compute", "AllGather", ALU.bypass,
         replica_groups=[list(range(NC))], ins=[ag_xn_in.opt()],
         outs=[xn_all.opt()], inc=(s_cc, 1))
    cc_xn1 = s_cc.n

    tp = tap("x1", [DSH, T])
    if tp is not None:
        y.wait(s_dy, s_dy.n)
        y.op("dma_start", out=tp, in_=x1_dram, inc=(s_dy, 16))
    tp = tap("xn1_all", [D, T], BF16)
    if tp is not None:
        y.wait(s_cc, cc_xn1)
        for r in range(16):
            y.op("dma_start", out=tp[r * 128:(r + 1) * 128, :],
                 in_=xn_all[r * 128:(r + 1) * 128, :], inc=(s_dy, 16))
    if upto <= 4:
        return finish()

    g.wait(s_v, s_v.n)
    g.wait(s_sc, s_sc.n)
    for kt in range(16):
        g.op("dma_start", out=wipv[:, kt, :], in_=w_inprojT_sh[kt * 128:(kt + 1) * 128, :], inc=(s_dg, 16))
    ld_wip = s_dg.n
    y.wait(s_v, s_v.n)
    y.wait(s_sc, s_sc.n)
    for i in range(NTILE1):
        y.op("dma_start", out=Acol[:, i, :], in_=A1_sh[i * 128:(i + 1) * 128, :], inc=(s_dy, 16))
        y.op("dma_start", out=Dcol[:, i:i + 1], in_=D1_sh[i * 128:(i + 1) * 128, :], inc=(s_dy, 16))
        y.op("dma_start", out=bcol[:, i:i + 1], in_=b_dt1_sh[i * 128:(i + 1) * 128, :], inc=(s_dy, 16))
    for i in range(NTILE1):
        g.op("dma_start", out=wxp1v[:, i, :], in_=w_xprojT1_sh[i * 128:(i + 1) * 128, :], inc=(s_dg, 16))
        g.op("dma_start", out=wdt1v[:, i, :], in_=w_dtprojT1_sh[:, i * 128:(i + 1) * 128], inc=(s_dg, 16))
        g.op("dma_start", out=woutv[:, i, :], in_=w_outprojT_sh[i * 128:(i + 1) * 128, :], inc=(s_dg, 16))
    ld_l1w = s_dg.n
    s.wait(s_dy, s_dy.n)
    for i in range(NTILE1):
        s.op("activation", out=Acol[:, i, :], in_=Acol[:, i, :], func=AF.Exp,
             inc=(s_sc, 1) if i == NTILE1 - 1 else None)
    v.wait(s_sc, s_sc.n)
    for i in range(NTILE1):
        v.op("tensor_scalar_mul", out=Acol[:, i, :], in0=Acol[:, i, :], scalar1=-1.0,
             inc=(s_v, 1) if i == NTILE1 - 1 else None)

    pmh = [pm[:, 2 * q:2 * q + 2, :].rearrange("p a b -> p (a b)") for q in range(4)]
    y.wait(s_cc, cc_xn1)
    for k in range(NCH):
        for sub in range(2):
            for kt in range(16):
                rp = rpA if kt % 2 == 0 else rpB
                y.wait(s_t, s_t.n)
                y.op("dma_start", out=rp[:, :], in_=xn_all[kt * 128:(kt + 1) * 128, k * TC:(k + 1) * TC], inc=(s_dy, 16))
                t.wait(s_dy, s_dy.n)
                if kt == 0:
                    t.wait(s_dg, ld_wip)
                    t.wait(s_sc, s_sc.n)
                    t.wait(s_v, s_v.n)
                for j in range(4):
                    t.op("matmul", pmh[j], wipv[:, kt, sub * 512 + j * 128:sub * 512 + (j + 1) * 128],
                         rp[:, :], start=(kt == 0), stop=(kt == 15),
                         inc=(s_t, 1) if j == 3 else None)
            s.wait(s_t, s_t.n)
            for j in range(4):
                s.wait(s_dy, s_dy.n)
                if sub == 0:
                    s.op("copy", out=xftile[:, :], in_=pmh[j], inc=(s_sc, 1))
                    y.wait(s_sc, s_sc.n)
                    y.op("dma_start", out=x1p_dram[j * 128:(j + 1) * 128, k * TC:(k + 1) * TC], in_=xftile[:, :], inc=(s_dy, 16))
                else:
                    s.op("copy", out=xntile[:, :], in_=pmh[j], inc=(s_sc, 1))
                    y.wait(s_sc, s_sc.n)
                    y.op("dma_start", out=z_dram[j * 128:(j + 1) * 128, k * TC:(k + 1) * TC], in_=xntile[:, :], inc=(s_dy, 16))
    inproj_done_t = s_t.n

    for i in range(NTILE1):
        y.wait(s_t, inproj_done_t if i == 0 else s_t.n)
        y.wait(s_sc, s_sc.n)
        y.wait(s_v, s_v.n)
        y.wait(s_dy, s_dy.n)
        y.op("dma_start", out=convc[:, :], in_=conv_sh[i * 128:(i + 1) * 128, :], inc=(s_dy, 16))
        y.op("dma_start", out=bigA[:, 3:3 + T], in_=x1p_dram[i * 128:(i + 1) * 128, :], inc=(s_dy, 16))
        v.wait(s_dy, s_dy.n)
        v.op("memset", bigA[:, 0:3], 0.0, inc=(s_v, 1))
        g.wait(s_dg, s_dg.n)
        for k in range(NCH):
            v.op("tensor_single_scalar", out=spt1[:, :], in_=bigA[:, k * TC:k * TC + TC],
                 scalar=convc[:, 0:1], op=ALU.mult)
            v.op("scalar_tensor_tensor", out=spt2[:, :], in0=bigA[:, k * TC + 1:k * TC + 1 + TC],
                 scalar=convc[:, 1:2], in1=spt1[:, :], op0=ALU.mult, op1=ALU.add)
            v.op("scalar_tensor_tensor", out=spt1[:, :], in0=bigA[:, k * TC + 2:k * TC + 2 + TC],
                 scalar=convc[:, 2:3], in1=spt2[:, :], op0=ALU.mult, op1=ALU.add)
            v.op("scalar_tensor_tensor", out=spt2[:, :], in0=bigA[:, k * TC + 3:k * TC + 3 + TC],
                 scalar=convc[:, 3:4], in1=spt1[:, :], op0=ALU.mult, op1=ALU.add, inc=(s_v, 1))
            s.wait(s_v, s_v.n)
            s.wait(s_dy, s_dy.n)
            s.wait(s_t, s_t.n)
            s.op("activation", out=uch[:, :], in_=spt2[:, :], func=AF.Silu, inc=(s_sc, 1))
            y.wait(s_sc, s_sc.n)
            y.op("dma_start", out=xc_dram[i * 128:(i + 1) * 128, k * TC:(k + 1) * TC], in_=uch[:, :], inc=(s_dy, 16))
            t.wait(s_sc, s_sc.n)
            t.wait(s_dg, ld_l1w)
            t.wait(s_v, s_v.n)
            t.op("matmul", pg[:, :], wxp1v[:, i, 0:128], uch[:, :], start=True, stop=True)
            t.op("matmul", pb2[0:32, :], wxp1v[:, i, 128:160], uch[:, :], start=True, stop=True, inc=(s_t, 1))
            v.wait(s_t, s_t.n)
            v.wait(s_dg, s_dg.n)
            v.op("tensor_copy", out=delta[:, :], in_=pg[:, :])
            v.op("tensor_copy", out=ych[0:32, :], in_=pb2[0:32, :], inc=(s_v, 1))
            g.wait(s_v, s_v.n)
            if i == 0:
                g.op("dma_start", out=xdbl1_in[0:128, k * TC:(k + 1) * TC], in_=delta[:, :], inc=(s_dg, 16))
                g.op("dma_start", out=xdbl1_in[128:160, k * TC:(k + 1) * TC], in_=ych[0:32, :], inc=(s_dg, 16))
            else:
                g.op("dma_start", out=xdbl1_in[0:128, k * TC:(k + 1) * TC], in_=delta[:, :],
                     accum_op=ALU.add, inc=(s_dg, 16))
                g.op("dma_start", out=xdbl1_in[128:160, k * TC:(k + 1) * TC], in_=ych[0:32, :],
                     accum_op=ALU.add, inc=(s_dg, 16))
    g.wait(s_dg, s_dg.n)
    g.op("collective_compute", "AllReduce", ALU.add,
         replica_groups=[list(range(NC))], ins=[xdbl1_in.opt()],
         outs=[xdbl1_out.opt()], inc=(s_cc, 1))
    cc_xdbl1 = s_cc.n
    g.wait(s_cc, cc_xdbl1)
    for k in range(NCH):
        g.wait(s_v, s_v.n)
        g.wait(s_dg, s_dg.n)
        g.op("dma_start", out=delta[:, :].bitcast(BF16)[:, 0:TC],
             in_=xdbl1_out[0:128, k * TC:(k + 1) * TC], inc=(s_dg, 16))
        g.op("dma_start", out=ych[0:32, :].bitcast(BF16)[:, 0:TC],
             in_=xdbl1_out[128:160, k * TC:(k + 1) * TC], inc=(s_dg, 16))
        g.wait(s_dg, s_dg.n)
        g.op("dma_start", out=xdbl1b[0:128, k * TC:(k + 1) * TC],
             in_=delta[:, :].bitcast(BF16)[:, 0:TC], inc=(s_dg, 16))
        g.op("dma_start", out=xdbl1b[128:160, k * TC:(k + 1) * TC],
             in_=ych[0:32, :].bitcast(BF16)[:, 0:TC], inc=(s_dg, 16))
    dg_xdbl1b = s_dg.n

    tp = tap("x1p", [DSH1, T])
    if tp is not None:
        y.wait(s_dy, s_dy.n)
        y.wait(s_dg, s_dg.n)
        y.op("dma_start", out=tp, in_=x1p_dram, inc=(s_dy, 16))
    tp = tap("xdbl1", [160, T])
    if tp is not None:
        y.wait(s_cc, cc_xdbl1)
        y.op("dma_start", out=tp, in_=xdbl1_out, inc=(s_dy, 16))
    tp = tap("xc", [DSH1, T], BF16)
    if tp is not None:
        y.wait(s_dy, s_dy.n)
        y.wait(s_dg, s_dg.n)
        y.op("dma_start", out=tp, in_=xc_dram, inc=(s_dy, 16))
    tp = tap("z1", [DSH1, T], BF16)
    if tp is not None:
        y.wait(s_dy, s_dy.n)
        y.op("dma_start", out=tp, in_=z_dram, inc=(s_dy, 16))
    if upto <= 5:
        return finish()

    def ffn_build(half):
        v.wait(s_t, s_t.n)
        for ht in range(16):
            for mat, stv, dstv in (("g", st_g, gshv), ("u", st_u, ushv)):
                for pair in range(2):
                    g.wait(s_v, s_v.n)
                    g.op("dma_start", out=tapA[:, :], in_=stv[2 * pair][half, ht], inc=(s_dg, 16))
                    g.op("dma_start", out=tapB[:, :], in_=stv[2 * pair + 1][half, ht], inc=(s_dg, 16))
                    y.wait(s_v, s_v.n)
                    y.op("dma_start", out=cof2[:, 0, :], in_=cf_g[2 * pair][half, ht], inc=(s_dy, 16))
                    y.op("dma_start", out=cof2[:, 1, :], in_=cf_g[2 * pair + 1][half, ht], inc=(s_dy, 16))
                    v.wait(s_dg, s_dg.n)
                    v.wait(s_dy, s_dy.n)
                    v.op("tensor_tensor", out=bz1[:, :], in0=tapA[:, :], in1=cof2[:, 0, :], op=ALU.mult)
                    v.op("tensor_tensor", out=bz2[:, :], in0=tapB[:, :], in1=cof2[:, 1, :], op=ALU.mult)
                    if pair == 0:
                        v.op("tensor_tensor", out=bz3[:, :], in0=bz1[:, :], in1=bz2[:, :], op=ALU.add, inc=(s_v, 1))
                    else:
                        v.op("tensor_tensor", out=bz1[:, :], in0=bz1[:, :], in1=bz2[:, :], op=ALU.add)
                        v.op("tensor_tensor", out=dstv[:, ht, :], in0=bz3[:, :], in1=bz1[:, :],
                             op=ALU.add, inc=(s_v, 1))
        for it in range(4):
            for piece in range(4):
                for pair in range(2):
                    g.wait(s_v, s_v.n)
                    g.op("dma_start", out=tapA[:, :],
                         in_=st_d[2 * pair][half, it, :, piece * 512:(piece + 1) * 512], inc=(s_dg, 16))
                    g.op("dma_start", out=tapB[:, :],
                         in_=st_d[2 * pair + 1][half, it, :, piece * 512:(piece + 1) * 512], inc=(s_dg, 16))
                    y.wait(s_v, s_v.n)
                    y.op("dma_start", out=cof2[:, 0, :],
                         in_=cf_d[2 * pair][half, it, :, piece * 512:(piece + 1) * 512], inc=(s_dy, 16))
                    y.op("dma_start", out=cof2[:, 1, :],
                         in_=cf_d[2 * pair + 1][half, it, :, piece * 512:(piece + 1) * 512], inc=(s_dy, 16))
                    v.wait(s_dg, s_dg.n)
                    v.wait(s_dy, s_dy.n)
                    v.op("tensor_tensor", out=bz1[:, :], in0=tapA[:, :], in1=cof2[:, 0, :], op=ALU.mult)
                    v.op("tensor_tensor", out=bz2[:, :], in0=tapB[:, :], in1=cof2[:, 1, :], op=ALU.mult)
                    if pair == 0:
                        v.op("tensor_tensor", out=bz3[:, :], in0=bz1[:, :], in1=bz2[:, :], op=ALU.add, inc=(s_v, 1))
                    else:
                        v.op("tensor_tensor", out=bz1[:, :], in0=bz1[:, :], in1=bz2[:, :], op=ALU.add)
                        v.op("tensor_tensor", out=dshv[:, it, piece * 512:(piece + 1) * 512],
                             in0=bz3[:, :], in1=bz1[:, :], op=ALU.add, inc=(s_v, 1))

    def ffn_chunk(half, k):
        y.wait(s_cc, cc_xn1)
        for itp in range(2):
            for kt in range(16):
                rp = rpA if kt % 2 == 0 else rpB
                y.wait(s_t, s_t.n)
                y.op("dma_start", out=rp[:, :], in_=xn_all[kt * 128:(kt + 1) * 128, k * TC:(k + 1) * TC], inc=(s_dy, 16))
                t.wait(s_dy, s_dy.n)
                if kt == 0:
                    t.wait(s_sc, s_sc.n)
                    t.wait(s_v, s_v.n)
                for q, (wsh, it) in enumerate(((gshv, 2 * itp), (gshv, 2 * itp + 1),
                                               (ushv, 2 * itp), (ushv, 2 * itp + 1))):
                    t.op("matmul", pmh[q], wsh[:, kt, it * 128:(it + 1) * 128], rp[:, :],
                         start=(kt == 0), stop=(kt == 15),
                         inc=(s_t, 1) if q == 3 else None)
            s.wait(s_t, s_t.n)
            for ii in range(2):
                it = 2 * itp + ii
                s.wait(s_v, s_v.n)
                s.op("activation", out=sgbuf[:, :], in_=pmh[ii], func=AF.Silu, inc=(s_sc, 1))
                v.wait(s_sc, s_sc.n)
                v.op("tensor_tensor", out=inter[:, it, :], in0=pmh[2 + ii], in1=sgbuf[:, :],
                     op=ALU.mult, inc=(s_v, 1))
        for ot in range(16):
            pt = pb1 if ot % 2 == 0 else pb2
            t.wait(s_sc, s_sc.n)
            t.wait(s_v, s_v.n)
            for kt in range(4):
                t.op("matmul", pt[:, :], dshv[:, kt, ot * 128:(ot + 1) * 128], inter[:, kt, :],
                     start=(kt == 0), stop=(kt == 3), inc=(s_t, 1) if kt == 3 else None)
            s.wait(s_t, s_t.n)
            s.wait(s_dg, s_dg.n)
            s.op("copy", out=xftile[:, :], in_=pt[:, :], inc=(s_sc, 1))
            g.wait(s_sc, s_sc.n)
            row0 = (ot // 2) * 512 + 256 + (ot % 2) * 128
            if half == 0:
                g.op("dma_start", out=rs_in[row0:row0 + 128, k * TC:(k + 1) * TC], in_=xftile[:, :], inc=(s_dg, 16))
            else:
                g.op("dma_start", out=rs_in[row0:row0 + 128, k * TC:(k + 1) * TC], in_=xftile[:, :],
                     accum_op=ALU.add, inc=(s_dg, 16))

    def l1_scan_chunk(k):
        y.wait(s_t, s_t.n)
        y.wait(s_dg, dg_xdbl1b)
        y.op("dma_start", out=dlc[:, :], in_=xdbl1b[0:128, k * TC:(k + 1) * TC], inc=(s_dy, 16))
        bcast_build(k, xdbl1b[128:144, :], xdbl1b[144:160, :])
        for i in range(NTILE1):
            t.wait(s_dy, s_dy.n)
            t.wait(s_sc, s_sc.n)
            t.op("matmul", pg[:, :], wdt1v[:, i, :], dlc[:, :], start=True, stop=True, inc=(s_t, 1))
            s.wait(s_t, s_t.n)
            s.wait(s_v, s_v.n)
            s.op("activation", out=spt1[:, :], in_=pg[:, :], func=AF.Abs, bias=bcol[:, i:i + 1])
            s.op("activation", out=spt2[:, :], in_=spt1[:, :], func=AF.Exp, scale=-1.0)
            s.op("activation", out=spt1[:, :], in_=spt2[:, :], func=AF.Ln, bias=1.0)
            s.op("activation", out=spt2[:, :], in_=pg[:, :], func=AF.Relu, bias=bcol[:, i:i + 1], inc=(s_sc, 1))
            v.wait(s_sc, s_sc.n)
            v.op("tensor_tensor", out=delta[:, :], in0=spt1[:, :], in1=spt2[:, :], op=ALU.add, inc=(s_v, 1))
            delta_done = s_v.n
            y.wait(s_v, s_v.n)
            y.wait(s_dy, s_dy.n)
            y.op("dma_start", out=uch[:, :], in_=xc_dram[i * 128:(i + 1) * 128, k * TC:(k + 1) * TC], inc=(s_dy, 16))
            y.op("dma_start", out=zch_l1[:, :], in_=z_dram[i * 128:(i + 1) * 128, k * TC:(k + 1) * TC], inc=(s_dy, 16))
            s.wait(s_v, delta_done)
            for n in range(DN):
                s.op("activation", out=dA[:, n, :], in_=delta[:, :], func=AF.Exp,
                     scale=Acol[:, i, n:n + 1], inc=(s_sc, 1) if n == DN - 1 else None)
            dA_done = s_sc.n
            v.wait(s_dy, s_dy.n)
            v.op("tensor_tensor", out=duch[:, :], in0=delta[:, :], in1=uch[:, :], op=ALU.mult)
            dub = duch[:, :].unsqueeze(1).broadcast_to([128, DN, TC])
            v.op("tensor_tensor", out=dBub[:, :, :], in0=dub, in1=Bbc[:, :, :], op=ALU.mult)
            v.wait(s_sc, dA_done)
            v.op("tensor_tensor", out=fix16[:, :], in0=dA[:, :, 0], in1=hprev[:, i, :], op=ALU.mult)
            v.op("tensor_tensor", out=dBub[:, :, 0], in0=dBub[:, :, 0], in1=fix16[:, :], op=ALU.add)
            v.op("memset", dA[:, :, 0], 0.0)
            v.op("tensor_tensor_scan", out=Hflat, data0=dAf, data1=dBuflat,
                 initial=0.0, op0=ALU.mult, op1=ALU.add, inc=(s_v, 1))
            v.wait(s_v, s_v.n)
            v.op("tensor_copy", out=hprev[:, i, :], in_=Hb[:, :, TC - 1])
            v.op("tensor_tensor", out=dBuflat, in0=Hflat, in1=Cbcflat, op=ALU.mult)
            v.op("tensor_tensor", out=Hb[:, 0:8, :], in0=dBub[:, 0:8, :], in1=dBub[:, 8:16, :], op=ALU.add)
            v.op("tensor_tensor", out=dBub[:, 0:4, :], in0=Hb[:, 0:4, :], in1=Hb[:, 4:8, :], op=ALU.add)
            v.op("tensor_tensor", out=Hb[:, 0:2, :], in0=dBub[:, 0:2, :], in1=dBub[:, 2:4, :], op=ALU.add)
            v.op("tensor_tensor", out=ych[:, :], in0=Hb[:, 0:1, :].rearrange("p a t -> p (a t)"),
                 in1=Hb[:, 1:2, :].rearrange("p a t -> p (a t)"), op=ALU.add)
            v.op("scalar_tensor_tensor", out=spt2[:, :], in0=uch[:, :], scalar=Dcol[:, i:i + 1],
                 in1=ych[:, :], op0=ALU.mult, op1=ALU.add, inc=(s_v, 1))
            s.wait(s_v, s_v.n)
            s.op("activation", out=zsil[:, :], in_=zch_l1[:, :], func=AF.Silu, inc=(s_sc, 1))
            v.wait(s_sc, s_sc.n)
            v.op("tensor_tensor", out=y3st[:, i, :], in0=spt2[:, :], in1=zsil[:, :],
                 op=ALU.mult, inc=(s_v, 1))
        for ot in range(16):
            pt = pb1 if ot % 2 == 0 else pb2
            t.wait(s_sc, s_sc.n)
            t.wait(s_v, s_v.n)
            for kt in range(NTILE1):
                t.op("matmul", pt[:, :], woutv[:, kt, ot * 128:(ot + 1) * 128], y3st[:, kt, :],
                     start=(kt == 0), stop=(kt == NTILE1 - 1), inc=(s_t, 1) if kt == NTILE1 - 1 else None)
            s.wait(s_t, s_t.n)
            s.wait(s_dy, s_dy.n)
            s.op("copy", out=xtile[:, :], in_=pt[:, :], inc=(s_sc, 1))
            y.wait(s_sc, s_sc.n)
            row0 = (ot // 2) * 512 + (ot % 2) * 128
            y.op("dma_start", out=rs_in[row0:row0 + 128, k * TC:(k + 1) * TC], in_=xtile[:, :], inc=(s_dy, 16))

    zch_l1 = xntile[:, :]

    v.wait(s_v, s_v.n)
    v.op("memset", hprev[:, :, :], 0.0, inc=(s_v, 1))
    ffn_build(0)
    sc_i = 0
    for i in range(32):
        h = 0 if i < 16 else 1
        if i == 16:
            ffn_build(1)
            g.wait(s_dg, s_dg.n)
        ffn_chunk(h, i % 16)
        if i % 2 == 1 and sc_i < NCH:
            l1_scan_chunk(sc_i)
            sc_i += 1
    while sc_i < NCH:
        l1_scan_chunk(sc_i)
        sc_i += 1

    g.wait(s_dy, s_dy.n)
    g.wait(s_dg, s_dg.n)
    g.op("collective_compute", "ReduceScatter", ALU.add,
         replica_groups=[list(range(NC))], ins=[rs_in.opt()],
         outs=[rs_out.opt()], inc=(s_cc, 1))
    cc_rs = s_cc.n
    y.wait(s_cc, cc_rs)
    for k in range(NCH):
        for i in range(NTILE):
            y.wait(s_v, s_v.n)
            y.op("dma_start", out=xtile[:, :], in_=rs_out[i * 128:(i + 1) * 128, k * TC:(k + 1) * TC], inc=(s_dy, 16))
            y.op("dma_start", out=spt1[:, :], in_=rs_out[DSH + i * 128:DSH + (i + 1) * 128, k * TC:(k + 1) * TC], inc=(s_dy, 16))
            y.op("dma_start", out=spt2[:, :], in_=x1_dram[i * 128:(i + 1) * 128, k * TC:(k + 1) * TC], inc=(s_dy, 16))
            s.wait(s_dy, s_dy.n)
            s.wait(s_v, s_v.n)
            s.op("activation", out=delta[:, :], in_=spt1[:, :], func=AF.Silu, inc=(s_sc, 1))
            v.wait(s_sc, s_sc.n)
            v.op("tensor_tensor", out=ych[:, :], in0=xtile[:, :], in1=delta[:, :], op=ALU.mult)
            v.op("tensor_tensor", out=xtile[:, :], in0=spt2[:, :], in1=ych[:, :], op=ALU.add, inc=(s_v, 1))
            y.wait(s_v, s_v.n)
            y.op("dma_start", out=out_sh[i * 128:(i + 1) * 128, k * TC:(k + 1) * TC], in_=xtile[:, :], inc=(s_dy, 16))
    return finish()


_BUILD_CACHE = {}


def kernel(**inputs):
    in_maps = prep_inputs(inputs)
    if "nc" not in _BUILD_CACHE:
        _BUILD_CACHE["nc"] = build()[0]
    nc = _BUILD_CACHE["nc"]
    res = run_bass_kernel_spmd(nc, in_maps, core_ids=list(range(NC)))
    shards = [np.asarray(res.results[c]["out_sh"]) for c in range(NC)]
    full = np.concatenate(shards, axis=0)
    return np.ascontiguousarray(full.T)[None].astype(np.float32)


# version 09
# speedup vs baseline: 1.2937x; 1.2937x over previous
import numpy as np
import ml_dtypes
import concourse.bass as bass
import concourse.mybir as mybir
from concourse.bass_utils import run_bass_kernel_spmd
from concourse.bass_core_rust import IndirectOffsetOnAxis

F32 = mybir.dt.float32
BF16 = mybir.dt.bfloat16
I32 = mybir.dt.int32
AF = mybir.ActivationFunctionType
ALU = mybir.AluOpType
Bft = ml_dtypes.bfloat16

NC = 8
T = 8192
D = 2048
DSH = D // NC
NTILE = DSH // 128
DI1 = 4096
DSH1 = DI1 // NC
NTILE1 = DSH1 // 128
DN = 16
NE = 8
CAP = 1280
TC = 512
NCH = T // TC
ISH = 8192 // NC
NCTRL = 130156
EPS = 1e-5
BIGS = float(1 << 20)
SENT = 1 << 30


_GRID_CACHE = {}


def _spline_grids():
    n = 8192 * 2048
    t = np.linspace(0.0, NCTRL - 1.0, n, dtype=np.float64)
    i = np.clip(np.floor(t).astype(np.int64), 0, NCTRL - 2)
    f = (t - i).astype(np.float32).astype(np.float64)
    idx = [np.clip(i + k, 0, NCTRL - 1).astype(np.int64) for k in (-1, 0, 1, 2)]
    f2 = f * f
    f3 = f2 * f
    coefs = [(0.5 * (-f + 2 * f2 - f3)).astype(np.float32),
             (0.5 * (2.0 - 5 * f2 + 3 * f3)).astype(np.float32),
             (0.5 * (f + 4 * f2 - 3 * f3)).astype(np.float32),
             (0.5 * (-f2 + f3)).astype(np.float32)]
    return idx, coefs


def _grids():
    if "g" not in _GRID_CACHE:
        _GRID_CACHE["g"] = _spline_grids()
    return _GRID_CACHE["g"]


def _prep_core(c, inp):
    f32 = lambda a: np.asarray(a, np.float32)
    d = {}
    x = f32(inp["L0_attn_x"][0])
    d["x_sh"] = np.ascontiguousarray(x.T[c * DSH:(c + 1) * DSH])
    d["w_gateT_sh"] = np.ascontiguousarray(
        f32(inp["L0_ffn_gate_weight"]).T[c * DSH:(c + 1) * DSH])
    d["w_xprojT_sh"] = np.ascontiguousarray(
        f32(inp["L0_attn_W_xproj"]).T[c * DSH:(c + 1) * DSH])
    d["w_dtT_sh"] = np.ascontiguousarray(
        f32(inp["L0_attn_W_dt"]).T[:, c * DSH:(c + 1) * DSH])
    d["b_dt_sh"] = np.ascontiguousarray(
        f32(inp["L0_attn_b_dt"])[c * DSH:(c + 1) * DSH]).reshape(DSH, 1)
    d["A0_sh"] = np.ascontiguousarray(f32(inp["L0_attn_A_log"])[c * DSH:(c + 1) * DSH])
    d["D0_sh"] = np.ascontiguousarray(f32(inp["L0_attn_D"])[c * DSH:(c + 1) * DSH]).reshape(DSH, 1)
    d["w1_e"] = np.ascontiguousarray(f32(inp["L0_ffn_w1"][c]))
    d["w2_e"] = np.ascontiguousarray(f32(inp["L0_ffn_w2"][c]))
    wip = f32(inp["L1_attn_in_proj_weight"])
    cols = np.concatenate([np.arange(c * DSH1, (c + 1) * DSH1),
                           np.arange(DI1 + c * DSH1, DI1 + (c + 1) * DSH1)])
    d["w_inprojT_sh"] = np.ascontiguousarray(wip[cols].T)
    d["conv_sh"] = np.ascontiguousarray(f32(inp["L1_attn_conv1d_weight"])[c * DSH1:(c + 1) * DSH1])
    d["w_xprojT1_sh"] = np.ascontiguousarray(
        f32(inp["L1_attn_x_proj_weight"]).T[c * DSH1:(c + 1) * DSH1])
    d["w_dtprojT1_sh"] = np.ascontiguousarray(
        f32(inp["L1_attn_dt_proj_weight"]).T[:, c * DSH1:(c + 1) * DSH1])
    d["b_dt1_sh"] = np.ascontiguousarray(
        f32(inp["L1_attn_dt_proj_bias"])[c * DSH1:(c + 1) * DSH1]).reshape(DSH1, 1)
    d["A1_sh"] = np.ascontiguousarray(f32(inp["L1_attn_A_log"])[c * DSH1:(c + 1) * DSH1])
    d["D1_sh"] = np.ascontiguousarray(f32(inp["L1_attn_D"])[c * DSH1:(c + 1) * DSH1]).reshape(DSH1, 1)
    d["w_outprojT_sh"] = np.ascontiguousarray(
        f32(inp["L1_attn_out_proj_weight"]).T[c * DSH1:(c + 1) * DSH1])
    idx, coefs = _grids()
    half_, ht_, p_, f_ = np.meshgrid(np.arange(2), np.arange(16), np.arange(128),
                                     np.arange(512), indexing="ij")
    j_gu = (ISH * c + 512 * half_ + f_) * 2048 + (128 * ht_ + p_)
    half2_, it_, p2_, h_ = np.meshgrid(np.arange(2), np.arange(4), np.arange(128),
                                       np.arange(2048), indexing="ij")
    j_d = h_ * 8192 + (ISH * c + 512 * half2_ + 128 * it_ + p2_)
    gcp = f32(inp["L1_ffn_gate_cp"])
    ucp = f32(inp["L1_ffn_up_cp"])
    dcp = f32(inp["L1_ffn_down_cp"])
    for k in range(4):
        gk = idx[k][j_gu]
        dk = idx[k][j_d]
        d[f"st_g{k}"] = gcp[gk]
        d[f"st_u{k}"] = ucp[gk]
        d[f"st_d{k}"] = dcp[dk]
        d[f"cf_g{k}"] = coefs[k][j_gu].astype(Bft)
        d[f"cf_d{k}"] = coefs[k][j_d].astype(Bft)
    d["ident"] = np.eye(128, dtype=Bft)
    d["identf"] = np.eye(128, dtype=np.float32)
    seln = np.zeros((16, 16 * 128), dtype=Bft)
    for n in range(16):
        seln[n, n * 128:(n + 1) * 128] = 1.0
    d["seln"] = seln
    tok = (np.arange(64)[None, :] * 128 + np.arange(128)[:, None]).astype(np.int32)
    d["tokidf"] = np.ascontiguousarray(tok.astype(np.float32))
    d["iota1280"] = np.broadcast_to(np.arange(CAP, dtype=np.float32), (128, CAP)).copy()
    d["myeoh"] = np.zeros((128, 8), np.float32)
    d["myeoh"][:, c] = 1.0
    return d


def prep_inputs(inputs):
    return [_prep_core(c, inputs) for c in range(NC)]



class Sem:
    __slots__ = ("h", "n", "_name")

    def __init__(self, name):
        self.h = None
        self.n = 0
        self._name = name

    def inc(self, k):
        self.n += k
        return self.n


class Q:
    def __init__(self, bld, key):
        self.b = bld
        self.key = key

    def op(self, name, *a, inc=None, **kw):
        if inc is not None:
            inc[0].inc(inc[1])

        import traceback
        def f(eng, name=name, a=a, kw=kw, inc=inc):
            try:
                ins = getattr(eng, name)(*a, **kw)
            except Exception:
                print(f"FAILED op {name} kw={list(kw.keys())} a={a}")
                raise
            if inc is not None:
                ins.then_inc(inc[0].h, inc[1])
        self.b.ops[self.key].append(f)

    def wait(self, sem, val=None):
        if val is None:
            val = sem.n

        def f(eng, sem=sem, val=val):
            eng.wait_ge(sem.h, val)
        self.b.ops[self.key].append(f)


class Bld:
    def __init__(self, nc):
        self.nc = nc
        self.ops = {"v": [], "s": [], "t": [], "g": [], "y": []}
        self.v = Q(self, "v")
        self.s = Q(self, "s")
        self.t = Q(self, "t")
        self.g = Q(self, "g")
        self.y = Q(self, "y")
        self.sems = []

    def sem(self, name):
        s = Sem(name)
        self.sems.append(s)
        return s


def _finalize(nc, b):
    import contextlib
    with contextlib.ExitStack() as st:
        for s in b.sems:
            s.h = st.enter_context(nc.semaphore(s._name))
        blk = st.enter_context(nc.Block())
        blk.vector(lambda e: [f(e) for f in b.ops["v"]])
        blk.scalar(lambda e: [f(e) for f in b.ops["s"]])
        blk.tensor(lambda e: [f(e) for f in b.ops["t"]])
        blk.gpsimd(lambda e: [f(e) for f in b.ops["g"]])
        blk.sync(lambda e: [f(e) for f in b.ops["y"]])
    return nc



def build(upto=99, taps=()):
    nc = bass.Bass("TRN2", target_bir_lowering=False, debug=False)
    b = Bld(nc)
    taps = set(taps)

    def ein(name, shape, dt=F32):
        return nc.dram_tensor(name, shape, dt, kind="ExternalInput").ap()

    def dint(name, shape, dt=F32):
        return nc.dram_tensor(name, shape, dt, kind="Internal").ap()

    x_sh = ein("x_sh", [DSH, T])
    w_gateT_sh = ein("w_gateT_sh", [DSH, 8])
    w_xprojT_sh = ein("w_xprojT_sh", [DSH, 96])
    w_dtT_sh = ein("w_dtT_sh", [64, DSH])
    b_dt_sh = ein("b_dt_sh", [DSH, 1])
    A0_sh = ein("A0_sh", [DSH, DN])
    D0_sh = ein("D0_sh", [DSH, 1])
    w1_e = ein("w1_e", [D, D])
    w2_e = ein("w2_e", [D, D])
    w_inprojT_sh = ein("w_inprojT_sh", [D, 2 * DSH1])
    conv_sh = ein("conv_sh", [DSH1, 4])
    w_xprojT1_sh = ein("w_xprojT1_sh", [DSH1, 160])
    w_dtprojT1_sh = ein("w_dtprojT1_sh", [128, DSH1])
    b_dt1_sh = ein("b_dt1_sh", [DSH1, 1])
    A1_sh = ein("A1_sh", [DSH1, DN])
    D1_sh = ein("D1_sh", [DSH1, 1])
    w_outprojT_sh = ein("w_outprojT_sh", [DSH1, D])
    st_g = [ein(f"st_g{k}", [2, 16, 128, 512]) for k in range(4)]
    st_u = [ein(f"st_u{k}", [2, 16, 128, 512]) for k in range(4)]
    st_d = [ein(f"st_d{k}", [2, 4, 128, 2048]) for k in range(4)]
    cf_g = [ein(f"cf_g{k}", [2, 16, 128, 512], BF16) for k in range(4)]
    cf_d = [ein(f"cf_d{k}", [2, 4, 128, 2048], BF16) for k in range(4)]
    ident = ein("ident", [128, 128], BF16)
    identf = ein("identf", [128, 128])
    seln = ein("seln", [16, 16 * 128], BF16)
    tokidf = ein("tokidf", [128, 64])
    iota1280 = ein("iota1280", [128, CAP])
    myeoh = ein("myeoh", [128, 8])

    out_sh = nc.dram_tensor("out_sh", [DSH, T], F32, kind="ExternalOutput").ap()

    cc_st_in = dint("cc_st_in", [2, T])
    cc_st_out = dint("cc_st_out", [2, T])
    cc_lg_in = dint("cc_lg_in", [8, T])
    cc_lg_out = dint("cc_lg_out", [8, T])
    ag_xn_in = dint("ag_xn_in", [DSH, T], BF16)
    xn_all = dint("xn_all", [D, T], BF16)
    ag_xnT_in = dint("ag_xnT_in", [T, DSH], BF16)
    xnT8_all = dint("xnT8_all", [NC * T, DSH], BF16)
    xdbl0_in = dint("xdbl0_in", [96, T])
    xdbl0_out = dint("xdbl0_out", [96, T])
    xdbl0b = dint("xdbl0b", [96, T], BF16)
    xnT_full = dint("xnT_full", [T, D], BF16)
    g_loc = dint("g_loc", [T, D], BF16)
    g_blk = dint("g_blk", [NC * T, DSH], BF16)
    g_rs = dint("g_rs", [T, DSH], BF16)
    w_dram = dint("w_dram", [1, T])
    inv_dram = dint("inv_dram", [1, CAP])
    s0_dram = dint("s0_dram", [DSH, T], BF16)
    x1_dram = dint("x1_dram", [DSH, T])
    xdbl1_in = dint("xdbl1_in", [160, T])
    xdbl1_out = dint("xdbl1_out", [160, T])
    xdbl1b = dint("xdbl1b", [160, T], BF16)
    x1p_dram = dint("x1p_dram", [DSH1, T])
    z_dram = dint("z_dram", [DSH1, T], BF16)
    xc_dram = dint("xc_dram", [DSH1, T], BF16)
    rs_in = dint("rs_in", [2 * D, T])
    rs_out = dint("rs_out", [2 * DSH, T])

    tap_aps = {}

    def tap(name, shape, dt=F32):
        if name in taps:
            tap_aps[name] = nc.dram_tensor("tap_" + name, shape, dt,
                                           kind="ExternalOutput").ap()
            return tap_aps[name]
        return None

    s_dy = b.sem("s_dy")
    s_dg = b.sem("s_dg")
    s_cc = b.sem("s_cc")
    s_v = b.sem("s_v")
    s_sc = b.sem("s_sc")
    s_t = b.sem("s_t")
    s_misc = b.sem("s_misc")

    def alloc(name, shape, dt):
        return nc.alloc_sbuf_tensor(name, shape, dt)

    A1f = alloc("A1f", [128, 8200], F32)
    A2b = alloc("A2b", [128, 11392], BF16)
    A3b = alloc("A3b", [128, 24576], BF16)
    Hb = alloc("Hb", [128, DN, TC], BF16)
    dBub = alloc("dBub", [128, DN, TC], BF16)
    Bbc = alloc("Bbc", [128, DN, TC], BF16)
    Cbc = alloc("Cbc", [128, DN, TC], BF16)
    delta = alloc("delta", [128, TC], F32)
    spt1 = alloc("spt1", [128, TC], F32)
    spt2 = alloc("spt2", [128, TC], F32)
    uch = alloc("uch", [128, TC], BF16)
    duch = alloc("duch", [128, TC], BF16)
    zsil = alloc("zsil", [128, TC], BF16)
    ych = alloc("ych", [128, TC], F32)
    fix16 = alloc("fix16", [128, DN], F32)
    hprev = alloc("hprev", [128, 4, DN], F32)
    Acol = alloc("Acol", [128, 4, DN], F32)
    Dcol = alloc("Dcol", [128, 4], F32)
    bcol = alloc("bcol", [128, 4], F32)
    Bchb = alloc("Bchb", [16, TC], BF16)
    Cchb = alloc("Cchb", [16, TC], BF16)
    dlc = alloc("dlc", [128, TC], BF16)
    onesf = alloc("onesf", [128, 1], F32)
    c_eps = alloc("c_eps", [128, 1], F32)
    nc.const_aps.register(F32, EPS, c_eps[:, :])
    nc.const_aps.register(F32, 1.0, onesf[:, :])
    onesr = alloc("onesr", [1, 128], F32)
    xtile = alloc("xtile", [128, TC], F32)
    xftile = alloc("xftile", [128, TC], F32)
    xntile = alloc("xntile", [128, TC], BF16)
    identt = alloc("identt", [128, 128], BF16)
    selnt = alloc("selnt", [16, 16 * 128], BF16)
    wxp0 = alloc("wxp0", [128, 2, 96], BF16)
    wdt0 = alloc("wdt0", [64, 2, 128], BF16)
    gws = alloc("gws", [128, 2, 8], F32)
    wv = alloc("wv", [128, 64], F32)
    invs = alloc("invs", [128, 10], I32)
    carry8 = alloc("carry8", [8, 1], F32)
    myeo = alloc("myeo", [128, 8], F32)
    convc = alloc("convc", [128, 4], F32)
    rpA = alloc("rpA", [128, TC], BF16)
    rpB = alloc("rpB", [128, TC], BF16)
    rpC = alloc("rpC", [128, TC], BF16)
    tapA = alloc("tapA", [128, 512], BF16)
    tapB = alloc("tapB", [128, 512], BF16)
    cof2 = alloc("cof2", [128, 2, 512], BF16)
    bz1 = alloc("bz1", [128, 512], BF16)
    bz2 = alloc("bz2", [128, 512], BF16)
    bz3 = alloc("bz3", [128, 512], BF16)
    sgbuf = alloc("sgbuf", [128, TC], BF16)
    A4b = alloc("A4b", [128, 2048], BF16)

    dA = A1f[:, 0:8192].rearrange("p (n t) -> p n t", n=DN)
    dAf = A1f[:, 0:8192]
    bigA = A1f[:, 0:8200]
    logT = A1f[:, 0:512]
    onehot = A1f[:, 512:1024]
    exf = A1f[:, 1024:1536]
    exclT = A1f[:, 1536:2048]
    keptT = A1f[:, 2048:2560]
    lgch = A1f[0:8, 2560:3072]
    mech = A1f[0:8, 3072:3584]
    inclch = A1f[0:8, 3584:4096]
    exclch = A1f[0:8, 4096:4608]
    ones8 = A1f[0:8, 4608:5120]
    strow = A1f[0:1, 5120:6144]
    identtf = A1f[:, 5632:5760]
    maxv = A1f[:, 7680:7744]
    sumex = A1f[:, 7744:7808]
    bigt = A1f[:, 7808:7872]
    kany = A1f[:, 7872:7936]
    slotv = A1f[:, 7936:8000]
    mrow = A1f[0:1, 6144:6656]
    vrow = A1f[0:1, 6656:7168]
    rrow = A1f[0:1, 7168:7680]
    wipv = A1f[:, 0:8192].bitcast(BF16).rearrange("p (k j) -> p k j", k=16)
    moe_rhs = A3b[:, 8192:12288].rearrange("p (k j) -> p k j", k=16)
    moe_hm = A3b[:, 12288:16384].rearrange("p (k j) -> p k j", k=16)
    moe_of = A3b[:, 16384:20480].rearrange("p (k j) -> p k j", k=16)
    woutv = A2b[:, 0:8192].rearrange("p (k j) -> p k j", k=4)
    wxp1v = A2b[:, 8192:8832].rearrange("p (k j) -> p k j", k=4)
    wdt1v = A2b[:, 8832:9344].rearrange("p (k j) -> p k j", k=4)
    inter = A2b[:, 9344:11392].rearrange("p (k j) -> p k j", k=4)
    wC = A3b[:, 20480:21504]
    wD = A3b[:, 21504:22528]
    sch = A4b[:, 0:512]
    gchb = A4b[:, 512:1536].rearrange("p (a b) -> p a b", a=2)
    y3st = A4b[:, 0:2048].rearrange("p (a b) -> p a b", a=4)
    gtile = A3b[:, 0:2048]
    wAv = A3b[:, 2048:4096]
    wBv = A3b[:, 4096:6144]
    tmaj = A3b[:, 6144:8192]
    gshv = A3b[:, 0:8192].rearrange("p (k j) -> p k j", k=16)
    ushv = A3b[:, 8192:16384].rearrange("p (k j) -> p k j", k=16)
    dshv = A3b[:, 16384:24576].rearrange("p (k j) -> p k j", k=4)

    pm = nc.alloc_psum_tensor("pm", [128, 8, 256], F32)
    pb1 = nc.alloc_psum_tensor("pb1", [128, 512], F32)
    pb2 = nc.alloc_psum_tensor("pb2", [128, 512], F32)
    pg = nc.alloc_psum_tensor("pg", [128, 512], F32)
    ptr = pm[:, 4:6, :].rearrange("p a b -> p (a b)").bitcast(BF16)[:, 0:128]
    ptr2 = pm[:, 6:8, :].rearrange("p a b -> p (a b)").bitcast(BF16)[:, 0:128]

    v, s, t, g, y = b.v, b.s, b.t, b.g, b.y

    def barrier():
        for q in (v, s, t, g, y):
            q.wait(s_v)
            q.wait(s_sc)
            q.wait(s_t)
            q.wait(s_dy)
            q.wait(s_dg)

    y.op("dma_start", out=identt[:, :], in_=ident, inc=(s_dy, 16))
    y.op("dma_start", out=selnt[:, :], in_=seln, inc=(s_dy, 16))
    y.op("dma_start", out=myeo[:, :], in_=myeoh, inc=(s_dy, 16))
    for i in range(NTILE):
        y.op("dma_start", out=Acol[:, i, :], in_=A0_sh[i * 128:(i + 1) * 128, :], inc=(s_dy, 16))
        y.op("dma_start", out=Dcol[:, i:i + 1], in_=D0_sh[i * 128:(i + 1) * 128, :], inc=(s_dy, 16))
        y.op("dma_start", out=bcol[:, i:i + 1], in_=b_dt_sh[i * 128:(i + 1) * 128, :], inc=(s_dy, 16))
        y.op("dma_start", out=gws[:, i, :], in_=w_gateT_sh[i * 128:(i + 1) * 128, :], inc=(s_dy, 16))
    ld_const = s_dy.n
    for i in range(NTILE):
        g.op("dma_start", out=wxp0[:, i, :], in_=w_xprojT_sh[i * 128:(i + 1) * 128, :], inc=(s_dg, 16))
        g.op("dma_start", out=wdt0[:, i, :], in_=w_dtT_sh[:, i * 128:(i + 1) * 128], inc=(s_dg, 16))
    ld_wx = s_dg.n

    v.op("memset", onesf[:, :], 1.0)
    v.op("memset", c_eps[:, :], EPS)
    v.op("memset", onesr[:, :], 1.0)
    v.op("memset", hprev[:, :, :], 0.0)
    v.op("memset", carry8[:, :], 0.0)
    v.op("memset", ones8, 1.0)
    v.op("memset", bigt[:, :], BIGS, inc=(s_v, 1))
    s.wait(s_dy, ld_const)
    for i in range(NTILE):
        s.op("activation", out=Acol[:, i, :], in_=Acol[:, i, :], func=AF.Exp,
             inc=(s_sc, 1) if i == NTILE - 1 else None)
    v.wait(s_sc, s_sc.n)
    for i in range(NTILE):
        v.op("tensor_scalar_mul", out=Acol[:, i, :], in0=Acol[:, i, :], scalar1=-1.0,
             inc=(s_v, 1) if i == NTILE - 1 else None)
    v_init_done = s_v.n
    s.wait(s_v, v_init_done)
    t.wait(s_v, v_init_done)
    g.wait(s_v, v_init_done)
    y.wait(s_v, v_init_done)

    for k in range(NCH):
        for i in range(NTILE):
            y.wait(s_t, s_t.n)
            y.op("dma_start", out=xtile[:, :], in_=x_sh[i * 128:(i + 1) * 128, k * TC:(k + 1) * TC], inc=(s_dy, 16))
            s.wait(s_dy, s_dy.n)
            s.wait(s_t, s_t.n)
            s.op("activation", out=spt1[:, :], in_=xtile[:, :], func=AF.Square, inc=(s_sc, 1))
            t.wait(s_dy, s_dy.n)
            if k > 0 and i == 0:
                t.wait(s_v, s_v.n)
            t.op("matmul", pg[0:1, :], onesf[:, 0:1], xtile[:, :],
                 start=(i == 0), stop=(i == NTILE - 1), inc=(s_t, 1))
            t.wait(s_sc, s_sc.n)
            t.op("matmul", pb1[0:1, :], onesf[:, 0:1], spt1[:, :],
                 start=(i == 0), stop=(i == NTILE - 1), inc=(s_t, 1))
        v.wait(s_t, s_t.n)
        v.wait(s_dy, s_dy.n)
        v.op("tensor_copy", out=strow[0:1, 0:TC], in_=pg[0:1, :])
        v.op("tensor_copy", out=strow[0:1, TC:2 * TC], in_=pb1[0:1, :], inc=(s_v, 1))
        y.wait(s_v, s_v.n)
        y.op("dma_start", out=cc_st_in[0:1, k * TC:(k + 1) * TC], in_=strow[0:1, 0:TC], inc=(s_dy, 16))
        y.op("dma_start", out=cc_st_in[1:2, k * TC:(k + 1) * TC], in_=strow[0:1, TC:2 * TC], inc=(s_dy, 16))
    g.wait(s_dy, s_dy.n)
    g.op("collective_compute", "AllReduce", ALU.add,
         replica_groups=[list(range(NC))], ins=[cc_st_in.opt()],
         outs=[cc_st_out.opt()], inc=(s_cc, 1))
    cc_stats0 = s_cc.n

    barrier()
    y.wait(s_cc, cc_stats0)
    xdbl_view = pm[0:96, 0:2, :].rearrange("p a b -> p (a b)")
    for k in range(NCH):
        y.wait(s_v, s_v.n)
        y.op("dma_start", out=strow[0:1, 0:TC], in_=cc_st_out[0:1, k * TC:(k + 1) * TC], inc=(s_dy, 16))
        y.op("dma_start", out=strow[0:1, TC:2 * TC], in_=cc_st_out[1:2, k * TC:(k + 1) * TC], inc=(s_dy, 16))
        v.wait(s_dy, s_dy.n)
        v.wait(s_sc, s_sc.n)
        v.op("tensor_scalar_mul", out=mrow[0:1, :], in0=strow[0:1, 0:TC], scalar1=1.0 / D)
        v.op("tensor_tensor", out=rrow[0:1, :], in0=mrow[0:1, :], in1=mrow[0:1, :], op=ALU.mult)
        v.op("scalar_tensor_tensor", out=vrow[0:1, :], in0=strow[0:1, TC:2 * TC],
             scalar=1.0 / D, in1=rrow[0:1, :], op0=ALU.mult, op1=ALU.subtract, inc=(s_v, 1))
        s.wait(s_v, s_v.n)
        s.op("activation", out=rrow[0:1, :], in_=vrow[0:1, :], func=AF.Ln, bias=EPS)
        s.op("activation", out=vrow[0:1, :], in_=rrow[0:1, :], func=AF.Exp, scale=-0.5, inc=(s_sc, 1))
        t.wait(s_v, s_v.n)
        t.op("matmul", pb2[:, :], onesr[0:1, :], mrow[0:1, :], start=True, stop=True)
        t.wait(s_sc, s_sc.n)
        t.op("matmul", pg[:, :], onesr[0:1, :], vrow[0:1, :], start=True, stop=True, inc=(s_t, 1))
        bcast_t = s_t.n
        for i in range(NTILE):
            y.wait(s_t, s_t.n)
            y.op("dma_start", out=xtile[:, :], in_=x_sh[i * 128:(i + 1) * 128, k * TC:(k + 1) * TC], inc=(s_dy, 16))
            v.wait(s_dy, s_dy.n)
            v.wait(s_t, s_t.n)
            v.op("tensor_tensor", out=xftile[:, :], in0=xtile[:, :], in1=pb2[:, :], op=ALU.subtract)
            v.op("tensor_tensor", out=spt1[:, :], in0=xftile[:, :], in1=pg[:, :], op=ALU.mult)
            v.op("tensor_copy", out=xntile[:, :], in_=spt1[:, :])
            v.op("tensor_copy", out=xftile[:, :], in_=spt1[:, :], inc=(s_v, 1))
            xn_ready = s_v.n
            y.wait(s_v, xn_ready)
            y.op("dma_start", out=ag_xn_in[i * 128:(i + 1) * 128, k * TC:(k + 1) * TC], in_=xntile[:, :], inc=(s_dy, 16))
            t.wait(s_v, xn_ready)
            t.op("matmul", pb1[0:8, :], gws[:, i, :], xftile[:, :],
                 start=(i == 0), stop=(i == NTILE - 1))
            t.wait(s_dg, ld_wx)
            t.op("matmul", xdbl_view, wxp0[:, i, :], xntile[:, :],
                 start=(i == 0), stop=(i == NTILE - 1), inc=(s_t, 1))
            for j in range(4):
                pt = ptr[:, :] if j % 2 == 0 else ptr2[:, :]
                t.wait(s_sc, s_sc.n)
                t.op("transpose", pt, xntile[:, j * 128:(j + 1) * 128], identt[:, :], inc=(s_t, 1))
                s.wait(s_t, s_t.n)
                s.wait(s_dy, s_dy.n)
                s.op("copy", out=tmaj[:, (i * 4 + j) * 128:(i * 4 + j + 1) * 128], in_=pt, inc=(s_sc, 1))
                y.wait(s_sc, s_sc.n)
                y.op("dma_start", out=ag_xnT_in[k * TC + j * 128:k * TC + (j + 1) * 128,
                                               i * 128:(i + 1) * 128],
                     in_=tmaj[:, (i * 4 + j) * 128:(i * 4 + j + 1) * 128], inc=(s_dy, 16))
        s.wait(s_t, s_t.n)
        s.wait(s_dy, s_dy.n)
        s.op("copy", out=lgch, in_=pb1[0:8, :], inc=(s_sc, 1))
        s.op("copy", out=spt2[0:96, :], in_=xdbl_view, inc=(s_sc, 1))
        y.wait(s_sc, s_sc.n)
        y.op("dma_start", out=cc_lg_in[:, k * TC:(k + 1) * TC], in_=lgch, inc=(s_dy, 16))
        y.op("dma_start", out=xdbl0_in[:, k * TC:(k + 1) * TC], in_=spt2[0:96, :], inc=(s_dy, 16))
    g.wait(s_dy, s_dy.n)
    g.op("collective_compute", "AllReduce", ALU.add,
         replica_groups=[list(range(NC))], ins=[cc_lg_in.opt()],
         outs=[cc_lg_out.opt()], inc=(s_cc, 1))
    cc_lg = s_cc.n
    g.op("collective_compute", "AllReduce", ALU.add,
         replica_groups=[list(range(NC))], ins=[xdbl0_in.opt()],
         outs=[xdbl0_out.opt()], inc=(s_cc, 1))
    cc_xdbl0 = s_cc.n
    g.op("collective_compute", "AllGather", ALU.bypass,
         replica_groups=[list(range(NC))], ins=[ag_xn_in.opt()],
         outs=[xn_all.opt()], inc=(s_cc, 1))
    cc_xn = s_cc.n
    g.op("collective_compute", "AllGather", ALU.bypass,
         replica_groups=[list(range(NC))], ins=[ag_xnT_in.opt()],
         outs=[xnT8_all.opt()], inc=(s_cc, 1))
    cc_xnT = s_cc.n

    g.wait(s_cc, cc_xdbl0)
    for k in range(NCH):
        g.op("dma_start", out=spt1[0:96, :].bitcast(BF16)[:, 0:TC],
             in_=xdbl0_out[:, k * TC:(k + 1) * TC], inc=(s_dg, 16))
        g.wait(s_dg, s_dg.n)
        g.op("dma_start", out=xdbl0b[:, k * TC:(k + 1) * TC],
             in_=spt1[0:96, :].bitcast(BF16)[:, 0:TC], inc=(s_dg, 16))
    dg_xdbl0b = s_dg.n

    tp = tap("xn_all", [D, T], BF16)
    if tp is not None:
        y.wait(s_cc, cc_xn)
        for r in range(16):
            y.op("dma_start", out=tp[r * 128:(r + 1) * 128, :],
                 in_=xn_all[r * 128:(r + 1) * 128, :], inc=(s_dy, 16))
    tp = tap("xdbl0", [96, T])
    if tp is not None:
        y.wait(s_cc, cc_xdbl0)
        y.op("dma_start", out=tp, in_=xdbl0_out, inc=(s_dy, 16))
    tp = tap("logits", [8, T])
    if tp is not None:
        y.wait(s_cc, cc_lg)
        y.op("dma_start", out=tp, in_=cc_lg_out, inc=(s_dy, 16))
    tp = tap("xnT8", [NC * T, DSH], BF16)
    if tp is not None:
        y.wait(s_cc, cc_xnT)
        for r in range(NC):
            y.op("dma_start", out=tp[r * T:(r + 1) * T, :],
                 in_=xnT8_all[r * T:(r + 1) * T, :], inc=(s_dy, 16))

    def finish():
        y.wait(s_dy, s_dy.n)
        y.wait(s_dg, s_dg.n)
        g.wait(s_dg, s_dg.n)
        return _finalize(nc, b), tap_aps

    if upto <= 1:
        return finish()

    barrier()
    onehot3 = onehot.rearrange("p (a b) -> p a b", b=8)
    logT3 = logT.rearrange("p (a b) -> p a b", b=8)
    exf3 = exf.rearrange("p (a b) -> p a b", b=8)
    keptT3 = keptT.rearrange("p (a b) -> p a b", b=8)
    exclT3 = exclT.rearrange("p (a b) -> p a b", b=8)
    y.wait(s_cc, cc_lg)
    y.wait(s_v, s_v.n)
    y.op("dma_start", out=identtf, in_=identf, inc=(s_dy, 16))
    for k in range(NCH):
        y.wait(s_sc, s_sc.n)
        y.op("dma_start", out=lgch, in_=cc_lg_out[:, k * TC:(k + 1) * TC], inc=(s_dy, 16))
        for j in range(4):
            pt = pb1[:, 0:128] if j % 2 == 0 else pb2[:, 0:128]
            t.wait(s_dy, s_dy.n)
            t.wait(s_sc, s_sc.n)
            t.op("transpose", pt[:, 0:8], lgch[0:8, j * 128:(j + 1) * 128], identtf[0:8, 0:8], inc=(s_t, 1))
            s.wait(s_t, s_t.n)
            s.op("copy", out=logT[:, (4 * k + j) * 8:(4 * k + j + 1) * 8], in_=pt[:, 0:8], inc=(s_sc, 1))
    v.wait(s_sc, s_sc.n)
    v.op("tensor_reduce", out=maxv[:, :].rearrange("p (a b) -> p a b", b=1), in_=logT3,
         axis=mybir.AxisListType.X, op=ALU.max)
    mxb = maxv[:, :].unsqueeze(2).broadcast_to([128, 64, 8])
    v.op("tensor_tensor", out=onehot3, in0=logT3, in1=mxb, op=ALU.is_ge)
    v.op("tensor_tensor", out=exf3, in0=logT3, in1=mxb, op=ALU.subtract, inc=(s_v, 1))
    s.wait(s_v, s_v.n)
    s.op("activation", out=exclT, in_=exf, func=AF.Exp, inc=(s_sc, 1))
    v.wait(s_sc, s_sc.n)
    v.op("tensor_reduce", out=sumex[:, :].rearrange("p (a b) -> p a b", b=1), in_=exclT3,
         axis=mybir.AxisListType.X, op=ALU.add)
    v.op("reciprocal", out=wv[:, :], in_=sumex[:, :], inc=(s_v, 1))
    for k in range(NCH):
        for j in range(4):
            pt = pb1[:, 0:128] if j % 2 == 0 else pb2[:, 0:128]
            t.wait(s_v, s_v.n)
            t.wait(s_sc, s_sc.n)
            t.op("transpose", pt[0:8, :], onehot[:, (4 * k + j) * 8:(4 * k + j + 1) * 8], identtf[:, :], inc=(s_t, 1))
            s.wait(s_t, s_t.n)
            s.op("copy", out=mech[:, j * 128:(j + 1) * 128], in_=pt[0:8, :], inc=(s_sc, 1))
        v.wait(s_sc, s_sc.n)
        v.wait(s_v, s_v.n)
        v.op("tensor_tensor_scan", out=inclch, data0=ones8, data1=mech,
             initial=carry8[:, 0:1], op0=ALU.mult, op1=ALU.add, inc=(s_v, 1))
        v.wait(s_v, s_v.n)
        v.op("tensor_copy", out=carry8[:, 0:1], in_=inclch[:, TC - 1:TC], inc=(s_v, 1))
        v.op("tensor_tensor", out=exclch, in0=inclch, in1=mech, op=ALU.subtract, inc=(s_v, 1))
        t.wait(s_v, s_v.n)
        t.wait(s_sc, s_sc.n)
        for j in range(4):
            t.op("transpose", pg[:, j * 8:(j + 1) * 8], exclch[0:8, j * 128:(j + 1) * 128],
                 identtf[0:8, 0:8], inc=(s_t, 1) if j == 3 else None)
        s.wait(s_t, s_t.n)
        s.op("copy", out=exclT[:, (4 * k) * 8:(4 * k + 4) * 8], in_=pg[:, 0:32], inc=(s_sc, 1))
    v.wait(s_sc, s_sc.n)
    v.op("tensor_single_scalar", out=exf, in_=exclT, scalar=float(CAP), op=ALU.is_lt)
    v.op("tensor_tensor", out=keptT, in0=exf, in1=onehot, op=ALU.mult)
    myb = myeo[:, :].unsqueeze(1).broadcast_to([128, 64, 8])
    v.op("tensor_tensor", out=exf3, in0=exclT3, in1=myb, op=ALU.mult)
    v.op("tensor_reduce", out=maxv[:, :].rearrange("p (a b) -> p a b", b=1), in_=exf3,
         axis=mybir.AxisListType.X, op=ALU.add)
    v.op("tensor_tensor", out=exf3, in0=keptT3, in1=myb, op=ALU.mult)
    v.op("tensor_reduce", out=slotv[:, :].rearrange("p (a b) -> p a b", b=1), in_=exf3,
         axis=mybir.AxisListType.X, op=ALU.add)
    v.op("tensor_tensor", out=bigt[:, :], in0=maxv[:, :], in1=slotv[:, :], op=ALU.mult)
    v.op("tensor_scalar", out=maxv[:, :], in0=slotv[:, :], scalar1=-BIGS, scalar2=BIGS,
         op0=ALU.mult, op1=ALU.add)
    v.op("tensor_tensor", out=kany[:, :], in0=bigt[:, :], in1=maxv[:, :], op=ALU.add,
         inc=(s_v, 1))
    idx_done = s_v.n
    iotav = A1f[:, 0:CAP]
    ohv = A1f[:, CAP:2 * CAP]
    tokf = A1f[:, 2 * CAP:2 * CAP + 64]
    y.wait(s_v, idx_done)
    y.op("dma_start", out=iotav, in_=iota1280, inc=(s_dy, 16))
    y.op("dma_start", out=tokf, in_=tokidf, inc=(s_dy, 16))
    inv_segs = [(pg[0:1, :], 0, 512), (pb1[0:1, :], 512, 512), (pb2[0:1, :], 1024, 256)]
    occ_segs = [(pm[0:1, 0:2, :].rearrange("p a b -> p (a b)"), 0, 512),
                (pm[0:1, 2:4, :].rearrange("p a b -> p (a b)"), 512, 512),
                (pm[0:1, 4:6, :].rearrange("p a b -> p (a b)"), 1024, 256)]
    t.wait(s_sc, s_sc.n)
    t.wait(s_dy, s_dy.n)
    tp_oh = tap("oh0", [128, CAP])
    tp_kany = tap("kany", [128, 64])
    for grp in range(64):
        v.wait(s_dy, s_dy.n)
        if grp > 0:
            v.wait(s_t, s_t.n)
        v.op("tensor_single_scalar", out=ohv, in_=iotav,
             scalar=kany[:, grp:grp + 1], op=ALU.is_equal, inc=(s_v, 1))
        if grp == 0 and tp_oh is not None:
            y.wait(s_v, s_v.n)
            y.op("dma_start", out=tp_oh, in_=ohv, inc=(s_dy, 16))
            y.op("dma_start", out=tp_kany, in_=kany[:, :], inc=(s_dy, 16))
            t.wait(s_dy, s_dy.n)
        t.wait(s_v, s_v.n)
        for (prow, off, ln) in inv_segs:
            t.op("matmul", prow[:, 0:ln], tokf[:, grp:grp + 1], ohv[:, off:off + ln],
                 start=(grp == 0), stop=(grp == 63))
        for qi, (prow, off, ln) in enumerate(occ_segs):
            t.op("matmul", prow[:, 0:ln], onesf[:, 0:1], ohv[:, off:off + ln],
                 start=(grp == 0), stop=(grp == 63),
                 inc=(s_t, 1) if qi == 2 else None)
    inv_rowS = A1f[0:1, 2560:2560 + CAP]
    occ_rowS = A1f[0:1, 3840:3840 + CAP]
    t2row = A1f[0:1, 5120:5120 + CAP]
    invf_row = A1f[0:1, 6400:6400 + CAP]
    s.wait(s_t, s_t.n)
    for (prow, off, ln) in inv_segs:
        s.op("copy", out=inv_rowS[:, off:off + ln], in_=prow[:, 0:ln])
    for qi, (prow, off, ln) in enumerate(occ_segs):
        s.op("copy", out=occ_rowS[:, off:off + ln], in_=prow[:, 0:ln],
             inc=(s_sc, 1) if qi == 2 else None)
    v.wait(s_sc, s_sc.n)
    v.op("tensor_scalar", out=t2row, in0=occ_rowS, scalar1=-BIGS, scalar2=BIGS,
         op0=ALU.mult, op1=ALU.add)
    v.op("tensor_tensor", out=invf_row, in0=inv_rowS, in1=t2row, op=ALU.add, inc=(s_v, 1))
    y.wait(s_v, s_v.n)
    y.op("dma_start", out=inv_dram, in_=invf_row, inc=(s_dy, 16))
    g.wait(s_dy, s_dy.n)
    g.op("dma_start", out=maxv[:, 0:10], in_=inv_dram.rearrange("o (q p) -> p (q o)", p=128),
         inc=(s_dg, 16), allow_slow_non_contiguous=True)
    inv_loaded = s_dg.n
    v.wait(s_dg, inv_loaded)
    v.op("tensor_copy", out=invs[:, :], in_=maxv[:, 0:10], inc=(s_v, 1))
    t.wait(s_v, s_v.n)
    t.op("transpose", pb1[0:64, 0:128], wv[:, :], identtf[:, :], inc=(s_t, 1))
    wrowS = A1f[0:64, 2560:2688]
    s.wait(s_t, s_t.n)
    s.op("copy", out=wrowS, in_=pb1[0:64, 0:128], inc=(s_sc, 1))
    y.wait(s_sc, s_sc.n)
    y.op("dma_start", out=w_dram.rearrange("o (a b) -> (o a) b", b=128), in_=wrowS, inc=(s_dy, 16))
    rt_done = s_v.n
    g.wait(s_cc, cc_xnT)
    for cb in range(NC):
        for seg in range(8):
            g.op("dma_start",
                 out=xnT_full[seg * 1024:(seg + 1) * 1024, cb * DSH:(cb + 1) * DSH],
                 in_=xnT8_all[cb * T + seg * 1024:cb * T + (seg + 1) * 1024, :],
                 inc=(s_dg, 16))
    xnTf_done = s_dg.n

    tp = tap("invd", [1, CAP])
    if tp is not None:
        y.wait(s_dg, inv_loaded)
        y.op("dma_start", out=tp, in_=inv_dram, inc=(s_dy, 16))
    tp = tap("occd", [1, CAP])
    if tp is not None:
        y.wait(s_dg, inv_loaded)
        y.op("dma_start", out=tp, in_=occ_rowS, inc=(s_dy, 16))
    tp = tap("invraw", [1, CAP])
    if tp is not None:
        y.wait(s_dg, inv_loaded)
        y.op("dma_start", out=tp, in_=inv_rowS, inc=(s_dy, 16))
    tp = tap("kanyf", [128, 64])
    if tp is not None:
        y.wait(s_dg, inv_loaded)
        y.op("dma_start", out=tp, in_=kany[:, :], inc=(s_dy, 16))
    tp = tap("inv", [128, 10], I32)
    if tp is not None:
        v.wait(s_v, rt_done)
        y.wait(s_v, rt_done)
        y.op("dma_start", out=tp, in_=invs[:, :], inc=(s_dy, 16))
    tp = tap("wvt", [128, 64])
    if tp is not None:
        y.wait(s_v, rt_done)
        y.op("dma_start", out=tp, in_=wv[:, :], inc=(s_dy, 16))
    if upto <= 2:
        return finish()

    NG = CAP // 256

    moe_state = {"gl": 0, "mmh": {}}

    def moe_group(grp):
        for sub in range(2):
            g.wait(s_t, s_t.n)
            g.wait(s_dg, xnTf_done)
            g.op("indirect_dma_start", out=gtile[:, :],
                 out_offset=None, in_=xnT_full,
                 in_offset=IndirectOffsetOnAxis(invs[:, 2 * grp + sub:2 * grp + sub + 1], 0),
                 bounds_check=T - 1, oob_is_err=False, inc=(s_dg, 16))
            for hb in range(16):
                pt = ptr[:, :] if hb % 2 == 0 else ptr2[:, :]
                t.wait(s_dg, s_dg.n)
                t.wait(s_sc, s_sc.n)
                t.op("transpose", pt, gtile[:, hb * 128:(hb + 1) * 128], identt[:, :], inc=(s_t, 1))
                s.wait(s_t, s_t.n)
                s.op("copy", out=moe_rhs[:, hb, sub * 128:(sub + 1) * 128], in_=pt, inc=(s_sc, 1))
        if grp == 0:
            tpr = tap("mrhs", [128, 16, 256], BF16)
            if tpr is not None:
                y.wait(s_sc, s_sc.n)
                y.op("dma_start", out=tpr.rearrange("p a b -> p (a b)"),
                     in_=moe_rhs[:, :, :].rearrange("p a b -> p (a b)"), inc=(s_dy, 16))
                t.wait(s_dy, s_dy.n)
        for ihalf in range(2):
            wvs = [wAv[:, 0:1024], wBv[:, 0:1024], wC[:, :], wD[:, :]]
            for kk in range(16):
                gw = moe_state["gl"]
                wv_ = wvs[gw % 4]
                if gw >= 4 and (gw - 4) in moe_state["mmh"]:
                    g.wait(s_t, moe_state["mmh"][gw - 4])
                else:
                    g.wait(s_t, s_t.n)
                g.op("dma_start", out=wv_, in_=w1_e[kk * 128:(kk + 1) * 128,
                                                    ihalf * 1024:(ihalf + 1) * 1024], inc=(s_dg, 16))
                t.wait(s_dg, s_dg.n)
                if kk == 0:
                    t.wait(s_sc, s_sc.n)
                for ii in range(8):
                    t.op("matmul", pm[:, ii, :], wv_[:, ii * 128:(ii + 1) * 128],
                         moe_rhs[:, kk, :], start=(kk == 0 and ii % 2 == 0),
                         stop=(kk == 15), inc=(s_t, 1) if ii == 7 else None)
                moe_state["mmh"][gw] = s_t.n
                moe_state["gl"] = gw + 1
            s.wait(s_t, s_t.n)
            for ii in range(8):
                s.op("activation", out=moe_hm[:, ihalf * 8 + ii, :], in_=pm[:, ii, :],
                     func=AF.Gelu, inc=(s_sc, 1))
        if grp == 0:
            tph = tap("mhm", [128, 16, 256], BF16)
            if tph is not None:
                y.wait(s_sc, s_sc.n)
                y.op("dma_start", out=tph.rearrange("p a b -> p (a b)"),
                     in_=moe_hm[:, :, :].rearrange("p a b -> p (a b)"), inc=(s_dy, 16))
                t.wait(s_dy, s_dy.n)
        for hhalf in range(2):
            wvs = [wAv[:, 0:1024], wBv[:, 0:1024], wC[:, :], wD[:, :]]
            for kk in range(16):
                gw = moe_state["gl"]
                wv_ = wvs[gw % 4]
                if gw >= 4 and (gw - 4) in moe_state["mmh"]:
                    g.wait(s_t, moe_state["mmh"][gw - 4])
                else:
                    g.wait(s_t, s_t.n)
                g.op("dma_start", out=wv_, in_=w2_e[kk * 128:(kk + 1) * 128,
                                                    hhalf * 1024:(hhalf + 1) * 1024], inc=(s_dg, 16))
                t.wait(s_dg, s_dg.n)
                if kk == 0:
                    t.wait(s_sc, s_sc.n)
                for hh in range(8):
                    t.op("matmul", pm[:, hh, :], wv_[:, hh * 128:(hh + 1) * 128],
                         moe_hm[:, kk, :], start=(kk == 0 and hh % 2 == 0),
                         stop=(kk == 15), inc=(s_t, 1) if hh == 7 else None)
                moe_state["mmh"][gw] = s_t.n
                moe_state["gl"] = gw + 1
            s.wait(s_t, s_t.n)
            for hh in range(8):
                s.op("copy", out=moe_of[:, hhalf * 8 + hh, :], in_=pm[:, hh, :], inc=(s_sc, 1))
        for sub in range(2):
            for hb in range(16):
                pt = ptr[:, :] if hb % 2 == 0 else ptr2[:, :]
                t.wait(s_sc, s_sc.n)
                t.wait(s_dg, s_dg.n)
                t.op("transpose", pt, moe_of[:, hb, sub * 128:(sub + 1) * 128], identt[:, :], inc=(s_t, 1))
                s.wait(s_t, s_t.n)
                s.op("copy", out=tmaj[:, hb * 128:(hb + 1) * 128], in_=pt, inc=(s_sc, 1))
            g.wait(s_sc, s_sc.n)
            g.wait(s_dy, gz_done)
            g.op("indirect_dma_start", out=g_loc,
                 out_offset=IndirectOffsetOnAxis(invs[:, 2 * grp + sub:2 * grp + sub + 1], 0),
                 in_=tmaj[:, :], in_offset=None,
                 bounds_check=T - 1, oob_is_err=False, inc=(s_dg, 16))

    dlc64 = dlc[0:64, :]

    def bcast_build(k, Bsrc, Csrc):
        y.wait(s_t, s_t.n)
        y.op("dma_start", out=Bchb[:, :], in_=Bsrc[:, k * TC:(k + 1) * TC], inc=(s_dy, 16))
        y.op("dma_start", out=Cchb[:, :], in_=Csrc[:, k * TC:(k + 1) * TC], inc=(s_dy, 16))
        t.wait(s_dy, s_dy.n)
        for n in range(DN):
            pt = pb1 if n % 2 == 0 else pb2
            t.wait(s_sc, s_sc.n)
            t.wait(s_v, s_v.n)
            t.op("matmul", pt[:, :], selnt[0:16, n * 128:(n + 1) * 128], Bchb[:, :],
                 start=True, stop=True, inc=(s_t, 1))
            s.wait(s_t, s_t.n)
            s.op("copy", out=Bbc[:, n, :], in_=pt[:, :], inc=(s_sc, 1))
        for n in range(DN):
            pt = pb1 if n % 2 == 0 else pb2
            t.wait(s_sc, s_sc.n)
            t.op("matmul", pt[:, :], selnt[0:16, n * 128:(n + 1) * 128], Cchb[:, :],
                 start=True, stop=True, inc=(s_t, 1))
            s.wait(s_t, s_t.n)
            s.op("copy", out=Cbc[:, n, :], in_=pt[:, :], inc=(s_sc, 1))

    Hflat = Hb[:, :, :].rearrange("p n t -> p (n t)")
    dBuflat = dBub[:, :, :].rearrange("p n t -> p (n t)")
    Cbcflat = Cbc[:, :, :].rearrange("p n t -> p (n t)")
    Bbcflat = Bbc[:, :, :].rearrange("p n t -> p (n t)")

    def l0_scan_chunk(k):
        y.wait(s_t, s_t.n)
        y.op("dma_start", out=dlc64, in_=xdbl0b[0:64, k * TC:(k + 1) * TC], inc=(s_dy, 16))
        bcast_build(k, xdbl0b[64:80, :], xdbl0b[80:96, :])
        for i in range(NTILE):
            t.wait(s_dy, s_dy.n)
            t.wait(s_sc, s_sc.n)
            t.op("matmul", pg[:, :], wdt0[:, i, :], dlc64, start=True, stop=True, inc=(s_t, 1))
            s.wait(s_t, s_t.n)
            s.wait(s_v, s_v.n)
            s.op("activation", out=spt1[:, :], in_=pg[:, :], func=AF.Abs, bias=bcol[:, i:i + 1])
            s.op("activation", out=spt2[:, :], in_=spt1[:, :], func=AF.Exp, scale=-1.0)
            s.op("activation", out=spt1[:, :], in_=spt2[:, :], func=AF.Ln, bias=1.0)
            s.op("activation", out=spt2[:, :], in_=pg[:, :], func=AF.Relu, bias=bcol[:, i:i + 1], inc=(s_sc, 1))
            v.wait(s_sc, s_sc.n)
            v.op("tensor_tensor", out=delta[:, :], in0=spt1[:, :], in1=spt2[:, :], op=ALU.add, inc=(s_v, 1))
            delta_done = s_v.n
            y.wait(s_v, s_v.n)
            y.wait(s_dy, s_dy.n)
            y.op("dma_start", out=uch[:, :], in_=ag_xn_in[i * 128:(i + 1) * 128, k * TC:(k + 1) * TC], inc=(s_dy, 16))
            s.wait(s_v, delta_done)
            for n in range(DN):
                s.op("activation", out=dA[:, n, :], in_=delta[:, :], func=AF.Exp,
                     scale=Acol[:, i, n:n + 1], inc=(s_sc, 1) if n == DN - 1 else None)
            dA_done = s_sc.n
            v.wait(s_dy, s_dy.n)
            v.op("tensor_tensor", out=duch[:, :], in0=delta[:, :], in1=uch[:, :], op=ALU.mult)
            dub = duch[:, :].unsqueeze(1).broadcast_to([128, DN, TC])
            v.op("tensor_tensor", out=dBub[:, :, :], in0=dub, in1=Bbc[:, :, :], op=ALU.mult)
            v.wait(s_sc, dA_done)
            v.op("tensor_tensor", out=fix16[:, :], in0=dA[:, :, 0], in1=hprev[:, i, :], op=ALU.mult)
            v.op("tensor_tensor", out=dBub[:, :, 0], in0=dBub[:, :, 0], in1=fix16[:, :], op=ALU.add)
            v.op("memset", dA[:, :, 0], 0.0)
            v.op("tensor_tensor_scan", out=Hflat, data0=dAf, data1=dBuflat,
                 initial=0.0, op0=ALU.mult, op1=ALU.add, inc=(s_v, 1))
            v.wait(s_v, s_v.n)
            v.op("tensor_copy", out=hprev[:, i, :], in_=Hb[:, :, TC - 1])
            v.op("tensor_tensor", out=dBuflat, in0=Hflat, in1=Cbcflat, op=ALU.mult)
            v.op("tensor_tensor", out=Hb[:, 0:8, :], in0=dBub[:, 0:8, :], in1=dBub[:, 8:16, :], op=ALU.add)
            v.op("tensor_tensor", out=dBub[:, 0:4, :], in0=Hb[:, 0:4, :], in1=Hb[:, 4:8, :], op=ALU.add)
            v.op("tensor_tensor", out=Hb[:, 0:2, :], in0=dBub[:, 0:2, :], in1=dBub[:, 2:4, :], op=ALU.add)
            v.op("tensor_tensor", out=ych[:, :], in0=Hb[:, 0:1, :].rearrange("p a t -> p (a t)"),
                 in1=Hb[:, 1:2, :].rearrange("p a t -> p (a t)"), op=ALU.add)
            v.op("scalar_tensor_tensor", out=sch[:, :], in0=uch[:, :], scalar=Dcol[:, i:i + 1],
                 in1=ych[:, :], op0=ALU.mult, op1=ALU.add, inc=(s_v, 1))
            y.wait(s_v, s_v.n)
            y.op("dma_start", out=s0_dram[i * 128:(i + 1) * 128, k * TC:(k + 1) * TC], in_=sch[:, :], inc=(s_dy, 16))

    barrier()
    v.wait(s_v, rt_done)
    v.op("memset", gtile, 0.0, inc=(s_v, 1))
    y.wait(s_v, s_v.n)
    for seg in range(64):
        y.op("dma_start", out=g_loc[seg * 128:(seg + 1) * 128, :], in_=gtile[:, :], inc=(s_dy, 16))
    gz_done = s_dy.n
    s.wait(s_v, rt_done)
    v.wait(s_dg, dg_xdbl0b)
    g.wait(s_dy, gz_done)
    for k in range(NCH):
        if k < NG:
            moe_group(k)
        l0_scan_chunk(k)
    g.wait(s_dg, s_dg.n)
    for cb in range(NC):
        for seg in range(8):
            g.op("dma_start",
                 out=g_blk[cb * T + seg * 1024:cb * T + (seg + 1) * 1024, :],
                 in_=g_loc[seg * 1024:(seg + 1) * 1024, cb * DSH:(cb + 1) * DSH],
                 inc=(s_dg, 16))
    g.wait(s_dg, s_dg.n)
    g.op("collective_compute", "ReduceScatter", ALU.add,
         replica_groups=[list(range(NC))], ins=[g_blk.opt()],
         outs=[g_rs.opt()], inc=(s_cc, 1))
    cc_eout = s_cc.n

    tp = tap("s0", [DSH, T], BF16)
    if tp is not None:
        y.wait(s_dy, s_dy.n)
        y.op("dma_start", out=tp, in_=s0_dram, inc=(s_dy, 16))
    tp = tap("xnTf", [T, D], BF16)
    if tp is not None:
        y.wait(s_dg, xnTf_done)
        for r in range(8):
            y.op("dma_start", out=tp[r * 1024:(r + 1) * 1024, :],
                 in_=xnT_full[r * 1024:(r + 1) * 1024, :], inc=(s_dy, 16))
    tp = tap("grs", [T, DSH], BF16)
    if tp is not None:
        y.wait(s_cc, cc_eout)
        y.op("dma_start", out=tp, in_=g_rs, inc=(s_dy, 16))
    if upto <= 3:
        return finish()

    barrier()
    for k in range(NCH):
        for i in range(NTILE):
            y.wait(s_cc, cc_eout)
            y.wait(s_v, s_v.n)
            y.op("dma_start", out=gchb[:, i, :],
                 in_=g_rs[k * TC:(k + 1) * TC, i * 128:(i + 1) * 128],
                 transpose=True, inc=(s_dy, 16))
        y.op("dma_start", out=ych[:, :],
             in_=w_dram[0:1, k * TC:(k + 1) * TC].broadcast_to([128, TC]),
             inc=(s_dy, 16))
        for i in range(NTILE):
            y.wait(s_t, s_t.n)
            y.op("dma_start", out=xtile[:, :], in_=x_sh[i * 128:(i + 1) * 128, k * TC:(k + 1) * TC], inc=(s_dy, 16))
            y.wait(s_v, s_v.n)
            y.wait(s_dy, s_dy.n)
            y.op("dma_start", out=sch[:, :], in_=s0_dram[i * 128:(i + 1) * 128, k * TC:(k + 1) * TC], inc=(s_dy, 16))
            v.wait(s_dy, s_dy.n)
            v.wait(s_sc, s_sc.n)
            v.op("tensor_tensor", out=spt1[:, :], in0=gchb[:, i, :], in1=ych[:, :], op=ALU.mult, inc=(s_v, 1))
            s.wait(s_v, s_v.n)
            s.op("activation", out=zsil[:, :], in_=spt1[:, :], func=AF.Silu, inc=(s_sc, 1))
            v.wait(s_sc, s_sc.n)
            v.op("tensor_tensor", out=spt1[:, :], in0=sch[:, :], in1=zsil[:, :], op=ALU.mult)
            v.op("tensor_tensor", out=spt2[:, :], in0=xtile[:, :], in1=spt1[:, :], op=ALU.add, inc=(s_v, 1))
            x1_done = s_v.n
            y.wait(s_v, x1_done)
            y.op("dma_start", out=x1_dram[i * 128:(i + 1) * 128, k * TC:(k + 1) * TC], in_=spt2[:, :], inc=(s_dy, 16))
            s.wait(s_v, x1_done)
            s.op("activation", out=spt1[:, :], in_=spt2[:, :], func=AF.Square, inc=(s_sc, 1))
            t.wait(s_v, x1_done)
            if k > 0 and i == 0:
                t.wait(s_v, s_v.n)
            t.op("matmul", pg[0:1, :], onesf[:, 0:1], spt2[:, :],
                 start=(i == 0), stop=(i == NTILE - 1), inc=(s_t, 1))
            t.wait(s_sc, s_sc.n)
            t.op("matmul", pb1[0:1, :], onesf[:, 0:1], spt1[:, :],
                 start=(i == 0), stop=(i == NTILE - 1), inc=(s_t, 1))
        v.wait(s_t, s_t.n)
        v.wait(s_dy, s_dy.n)
        v.op("tensor_copy", out=strow[0:1, 0:TC], in_=pg[0:1, :])
        v.op("tensor_copy", out=strow[0:1, TC:2 * TC], in_=pb1[0:1, :], inc=(s_v, 1))
        y.wait(s_v, s_v.n)
        y.op("dma_start", out=cc_st_in[0:1, k * TC:(k + 1) * TC], in_=strow[0:1, 0:TC], inc=(s_dy, 16))
        y.op("dma_start", out=cc_st_in[1:2, k * TC:(k + 1) * TC], in_=strow[0:1, TC:2 * TC], inc=(s_dy, 16))
    g.wait(s_dy, s_dy.n)
    g.op("collective_compute", "AllReduce", ALU.add,
         replica_groups=[list(range(NC))], ins=[cc_st_in.opt()],
         outs=[cc_st_out.opt()], inc=(s_cc, 1))
    cc_stats1 = s_cc.n

    y.wait(s_cc, cc_stats1)
    for k in range(NCH):
        y.wait(s_v, s_v.n)
        y.op("dma_start", out=strow[0:1, 0:TC], in_=cc_st_out[0:1, k * TC:(k + 1) * TC], inc=(s_dy, 16))
        y.op("dma_start", out=strow[0:1, TC:2 * TC], in_=cc_st_out[1:2, k * TC:(k + 1) * TC], inc=(s_dy, 16))
        v.wait(s_dy, s_dy.n)
        v.wait(s_sc, s_sc.n)
        v.op("tensor_scalar_mul", out=mrow[0:1, :], in0=strow[0:1, 0:TC], scalar1=1.0 / D)
        v.op("tensor_tensor", out=rrow[0:1, :], in0=mrow[0:1, :], in1=mrow[0:1, :], op=ALU.mult)
        v.op("scalar_tensor_tensor", out=vrow[0:1, :], in0=strow[0:1, TC:2 * TC],
             scalar=1.0 / D, in1=rrow[0:1, :], op0=ALU.mult, op1=ALU.subtract, inc=(s_v, 1))
        s.wait(s_v, s_v.n)
        s.op("activation", out=rrow[0:1, :], in_=vrow[0:1, :], func=AF.Ln, bias=EPS)
        s.op("activation", out=vrow[0:1, :], in_=rrow[0:1, :], func=AF.Exp, scale=-0.5, inc=(s_sc, 1))
        t.wait(s_v, s_v.n)
        t.op("matmul", pb2[:, :], onesr[0:1, :], mrow[0:1, :], start=True, stop=True)
        t.wait(s_sc, s_sc.n)
        t.op("matmul", pg[:, :], onesr[0:1, :], vrow[0:1, :], start=True, stop=True, inc=(s_t, 1))
        for i in range(NTILE):
            y.wait(s_t, s_t.n)
            y.op("dma_start", out=xtile[:, :], in_=x1_dram[i * 128:(i + 1) * 128, k * TC:(k + 1) * TC], inc=(s_dy, 16))
            v.wait(s_dy, s_dy.n)
            v.wait(s_t, s_t.n)
            v.op("tensor_tensor", out=spt1[:, :], in0=xtile[:, :], in1=pb2[:, :], op=ALU.subtract)
            v.op("tensor_tensor", out=xntile[:, :], in0=spt1[:, :], in1=pg[:, :], op=ALU.mult, inc=(s_v, 1))
            y.wait(s_v, s_v.n)
            y.op("dma_start", out=ag_xn_in[i * 128:(i + 1) * 128, k * TC:(k + 1) * TC], in_=xntile[:, :], inc=(s_dy, 16))
    g.wait(s_dy, s_dy.n)
    g.op("collective_compute", "AllGather", ALU.bypass,
         replica_groups=[list(range(NC))], ins=[ag_xn_in.opt()],
         outs=[xn_all.opt()], inc=(s_cc, 1))
    cc_xn1 = s_cc.n

    tp = tap("x1", [DSH, T])
    if tp is not None:
        y.wait(s_dy, s_dy.n)
        y.op("dma_start", out=tp, in_=x1_dram, inc=(s_dy, 16))
    tp = tap("xn1_all", [D, T], BF16)
    if tp is not None:
        y.wait(s_cc, cc_xn1)
        for r in range(16):
            y.op("dma_start", out=tp[r * 128:(r + 1) * 128, :],
                 in_=xn_all[r * 128:(r + 1) * 128, :], inc=(s_dy, 16))
    if upto <= 4:
        return finish()

    barrier()
    g.wait(s_v, s_v.n)
    g.wait(s_sc, s_sc.n)
    for kt in range(16):
        g.op("dma_start", out=wipv[:, kt, :], in_=w_inprojT_sh[kt * 128:(kt + 1) * 128, :], inc=(s_dg, 16))
    ld_wip = s_dg.n
    y.wait(s_v, s_v.n)
    y.wait(s_sc, s_sc.n)
    for i in range(NTILE1):
        y.op("dma_start", out=Acol[:, i, :], in_=A1_sh[i * 128:(i + 1) * 128, :], inc=(s_dy, 16))
        y.op("dma_start", out=Dcol[:, i:i + 1], in_=D1_sh[i * 128:(i + 1) * 128, :], inc=(s_dy, 16))
        y.op("dma_start", out=bcol[:, i:i + 1], in_=b_dt1_sh[i * 128:(i + 1) * 128, :], inc=(s_dy, 16))
    for i in range(NTILE1):
        g.op("dma_start", out=wxp1v[:, i, :], in_=w_xprojT1_sh[i * 128:(i + 1) * 128, :], inc=(s_dg, 16))
        g.op("dma_start", out=wdt1v[:, i, :], in_=w_dtprojT1_sh[:, i * 128:(i + 1) * 128], inc=(s_dg, 16))
        g.op("dma_start", out=woutv[:, i, :], in_=w_outprojT_sh[i * 128:(i + 1) * 128, :], inc=(s_dg, 16))
    ld_l1w = s_dg.n
    s.wait(s_dy, s_dy.n)
    for i in range(NTILE1):
        s.op("activation", out=Acol[:, i, :], in_=Acol[:, i, :], func=AF.Exp,
             inc=(s_sc, 1) if i == NTILE1 - 1 else None)
    v.wait(s_sc, s_sc.n)
    for i in range(NTILE1):
        v.op("tensor_scalar_mul", out=Acol[:, i, :], in0=Acol[:, i, :], scalar1=-1.0,
             inc=(s_v, 1) if i == NTILE1 - 1 else None)

    pmh = [pm[:, 2 * q:2 * q + 2, :].rearrange("p a b -> p (a b)") for q in range(4)]
    y.wait(s_cc, cc_xn1)
    rps = [rpA, rpB, rpC]
    mmh = {}
    gl = 0
    for k in range(NCH):
        for sub in range(2):
            for kt in range(16):
                rp = rps[gl % 3]
                if gl >= 3:
                    y.wait(s_t, mmh[gl - 3])
                y.op("dma_start", out=rp[:, :], in_=xn_all[kt * 128:(kt + 1) * 128, k * TC:(k + 1) * TC], inc=(s_dy, 16))
                t.wait(s_dy, s_dy.n)
                if kt == 0:
                    t.wait(s_dg, ld_wip)
                    t.wait(s_sc, s_sc.n)
                    t.wait(s_v, s_v.n)
                for j in range(4):
                    t.op("matmul", pmh[j], wipv[:, kt, sub * 512 + j * 128:sub * 512 + (j + 1) * 128],
                         rp[:, :], start=(kt == 0), stop=(kt == 15),
                         inc=(s_t, 1) if j == 3 else None)
                mmh[gl] = s_t.n
                gl += 1
            s.wait(s_t, s_t.n)
            for j in range(4):
                s.wait(s_dy, s_dy.n)
                if sub == 0:
                    s.op("copy", out=xftile[:, :], in_=pmh[j], inc=(s_sc, 1))
                    y.wait(s_sc, s_sc.n)
                    y.op("dma_start", out=x1p_dram[j * 128:(j + 1) * 128, k * TC:(k + 1) * TC], in_=xftile[:, :], inc=(s_dy, 16))
                else:
                    s.op("copy", out=xntile[:, :], in_=pmh[j], inc=(s_sc, 1))
                    y.wait(s_sc, s_sc.n)
                    y.op("dma_start", out=z_dram[j * 128:(j + 1) * 128, k * TC:(k + 1) * TC], in_=xntile[:, :], inc=(s_dy, 16))
    inproj_done_t = s_t.n

    barrier()
    for i in range(NTILE1):
        y.wait(s_t, inproj_done_t if i == 0 else s_t.n)
        y.wait(s_sc, s_sc.n)
        y.wait(s_v, s_v.n)
        y.wait(s_dy, s_dy.n)
        y.op("dma_start", out=convc[:, :], in_=conv_sh[i * 128:(i + 1) * 128, :], inc=(s_dy, 16))
        y.op("dma_start", out=bigA[:, 3:3 + T], in_=x1p_dram[i * 128:(i + 1) * 128, :], inc=(s_dy, 16))
        v.wait(s_dy, s_dy.n)
        v.op("memset", bigA[:, 0:3], 0.0, inc=(s_v, 1))
        g.wait(s_dg, s_dg.n)
        for k in range(NCH):
            v.op("tensor_single_scalar", out=spt1[:, :], in_=bigA[:, k * TC:k * TC + TC],
                 scalar=convc[:, 0:1], op=ALU.mult)
            v.op("scalar_tensor_tensor", out=spt2[:, :], in0=bigA[:, k * TC + 1:k * TC + 1 + TC],
                 scalar=convc[:, 1:2], in1=spt1[:, :], op0=ALU.mult, op1=ALU.add)
            v.op("scalar_tensor_tensor", out=spt1[:, :], in0=bigA[:, k * TC + 2:k * TC + 2 + TC],
                 scalar=convc[:, 2:3], in1=spt2[:, :], op0=ALU.mult, op1=ALU.add)
            v.op("scalar_tensor_tensor", out=spt2[:, :], in0=bigA[:, k * TC + 3:k * TC + 3 + TC],
                 scalar=convc[:, 3:4], in1=spt1[:, :], op0=ALU.mult, op1=ALU.add, inc=(s_v, 1))
            s.wait(s_v, s_v.n)
            s.wait(s_dy, s_dy.n)
            s.wait(s_t, s_t.n)
            s.op("activation", out=uch[:, :], in_=spt2[:, :], func=AF.Silu, inc=(s_sc, 1))
            y.wait(s_sc, s_sc.n)
            y.op("dma_start", out=xc_dram[i * 128:(i + 1) * 128, k * TC:(k + 1) * TC], in_=uch[:, :], inc=(s_dy, 16))
            t.wait(s_sc, s_sc.n)
            t.wait(s_dg, ld_l1w)
            t.wait(s_v, s_v.n)
            t.op("matmul", pg[:, :], wxp1v[:, i, 0:128], uch[:, :], start=True, stop=True)
            t.op("matmul", pb2[0:32, :], wxp1v[:, i, 128:160], uch[:, :], start=True, stop=True, inc=(s_t, 1))
            v.wait(s_t, s_t.n)
            v.wait(s_dg, s_dg.n)
            v.op("tensor_copy", out=delta[:, :], in_=pg[:, :])
            v.op("tensor_copy", out=ych[0:32, :], in_=pb2[0:32, :], inc=(s_v, 1))
            g.wait(s_v, s_v.n)
            if i == 0:
                g.op("dma_start", out=xdbl1_in[0:128, k * TC:(k + 1) * TC], in_=delta[:, :], inc=(s_dg, 16))
                g.op("dma_start", out=xdbl1_in[128:160, k * TC:(k + 1) * TC], in_=ych[0:32, :], inc=(s_dg, 16))
            else:
                g.op("dma_start", out=xdbl1_in[0:128, k * TC:(k + 1) * TC], in_=delta[:, :],
                     accum_op=ALU.add, inc=(s_dg, 16))
                g.op("dma_start", out=xdbl1_in[128:160, k * TC:(k + 1) * TC], in_=ych[0:32, :],
                     accum_op=ALU.add, inc=(s_dg, 16))
    g.wait(s_dg, s_dg.n)
    g.op("collective_compute", "AllReduce", ALU.add,
         replica_groups=[list(range(NC))], ins=[xdbl1_in.opt()],
         outs=[xdbl1_out.opt()], inc=(s_cc, 1))
    cc_xdbl1 = s_cc.n
    g.wait(s_cc, cc_xdbl1)
    for k in range(NCH):
        g.wait(s_v, s_v.n)
        g.wait(s_dg, s_dg.n)
        g.op("dma_start", out=delta[:, :].bitcast(BF16)[:, 0:TC],
             in_=xdbl1_out[0:128, k * TC:(k + 1) * TC], inc=(s_dg, 16))
        g.op("dma_start", out=ych[0:32, :].bitcast(BF16)[:, 0:TC],
             in_=xdbl1_out[128:160, k * TC:(k + 1) * TC], inc=(s_dg, 16))
        g.wait(s_dg, s_dg.n)
        g.op("dma_start", out=xdbl1b[0:128, k * TC:(k + 1) * TC],
             in_=delta[:, :].bitcast(BF16)[:, 0:TC], inc=(s_dg, 16))
        g.op("dma_start", out=xdbl1b[128:160, k * TC:(k + 1) * TC],
             in_=ych[0:32, :].bitcast(BF16)[:, 0:TC], inc=(s_dg, 16))
    dg_xdbl1b = s_dg.n

    tp = tap("x1p", [DSH1, T])
    if tp is not None:
        y.wait(s_dy, s_dy.n)
        y.wait(s_dg, s_dg.n)
        y.op("dma_start", out=tp, in_=x1p_dram, inc=(s_dy, 16))
    tp = tap("xdbl1", [160, T])
    if tp is not None:
        y.wait(s_cc, cc_xdbl1)
        y.op("dma_start", out=tp, in_=xdbl1_out, inc=(s_dy, 16))
    tp = tap("xc", [DSH1, T], BF16)
    if tp is not None:
        y.wait(s_dy, s_dy.n)
        y.wait(s_dg, s_dg.n)
        y.op("dma_start", out=tp, in_=xc_dram, inc=(s_dy, 16))
    tp = tap("z1", [DSH1, T], BF16)
    if tp is not None:
        y.wait(s_dy, s_dy.n)
        y.op("dma_start", out=tp, in_=z_dram, inc=(s_dy, 16))
    if upto <= 5:
        return finish()

    barrier()
    ffn_state = {"gl": 0, "mmh": {}}

    def ffn_build(half):
        v.wait(s_t, s_t.n)
        for ht in range(16):
            for mat, stv, dstv in (("g", st_g, gshv), ("u", st_u, ushv)):
                for pair in range(2):
                    g.wait(s_v, s_v.n)
                    g.op("dma_start", out=tapA[:, :], in_=stv[2 * pair][half, ht], inc=(s_dg, 16))
                    g.op("dma_start", out=tapB[:, :], in_=stv[2 * pair + 1][half, ht], inc=(s_dg, 16))
                    y.wait(s_v, s_v.n)
                    y.op("dma_start", out=cof2[:, 0, :], in_=cf_g[2 * pair][half, ht], inc=(s_dy, 16))
                    y.op("dma_start", out=cof2[:, 1, :], in_=cf_g[2 * pair + 1][half, ht], inc=(s_dy, 16))
                    v.wait(s_dg, s_dg.n)
                    v.wait(s_dy, s_dy.n)
                    v.op("tensor_tensor", out=bz1[:, :], in0=tapA[:, :], in1=cof2[:, 0, :], op=ALU.mult)
                    v.op("tensor_tensor", out=bz2[:, :], in0=tapB[:, :], in1=cof2[:, 1, :], op=ALU.mult)
                    if pair == 0:
                        v.op("tensor_tensor", out=bz3[:, :], in0=bz1[:, :], in1=bz2[:, :], op=ALU.add, inc=(s_v, 1))
                    else:
                        v.op("tensor_tensor", out=bz1[:, :], in0=bz1[:, :], in1=bz2[:, :], op=ALU.add)
                        v.op("tensor_tensor", out=dstv[:, ht, :], in0=bz3[:, :], in1=bz1[:, :],
                             op=ALU.add, inc=(s_v, 1))
        for it in range(4):
            for piece in range(4):
                for pair in range(2):
                    g.wait(s_v, s_v.n)
                    g.op("dma_start", out=tapA[:, :],
                         in_=st_d[2 * pair][half, it, :, piece * 512:(piece + 1) * 512], inc=(s_dg, 16))
                    g.op("dma_start", out=tapB[:, :],
                         in_=st_d[2 * pair + 1][half, it, :, piece * 512:(piece + 1) * 512], inc=(s_dg, 16))
                    y.wait(s_v, s_v.n)
                    y.op("dma_start", out=cof2[:, 0, :],
                         in_=cf_d[2 * pair][half, it, :, piece * 512:(piece + 1) * 512], inc=(s_dy, 16))
                    y.op("dma_start", out=cof2[:, 1, :],
                         in_=cf_d[2 * pair + 1][half, it, :, piece * 512:(piece + 1) * 512], inc=(s_dy, 16))
                    v.wait(s_dg, s_dg.n)
                    v.wait(s_dy, s_dy.n)
                    v.op("tensor_tensor", out=bz1[:, :], in0=tapA[:, :], in1=cof2[:, 0, :], op=ALU.mult)
                    v.op("tensor_tensor", out=bz2[:, :], in0=tapB[:, :], in1=cof2[:, 1, :], op=ALU.mult)
                    if pair == 0:
                        v.op("tensor_tensor", out=bz3[:, :], in0=bz1[:, :], in1=bz2[:, :], op=ALU.add, inc=(s_v, 1))
                    else:
                        v.op("tensor_tensor", out=bz1[:, :], in0=bz1[:, :], in1=bz2[:, :], op=ALU.add)
                        v.op("tensor_tensor", out=dshv[:, it, piece * 512:(piece + 1) * 512],
                             in0=bz3[:, :], in1=bz1[:, :], op=ALU.add, inc=(s_v, 1))

    def ffn_chunk(half, k):
        y.wait(s_cc, cc_xn1)
        rps = [rpA, rpB, rpC]
        for itp in range(2):
            for kt in range(16):
                gl = ffn_state["gl"]
                rp = rps[gl % 3]
                if gl >= 3 and (gl - 3) in ffn_state["mmh"]:
                    y.wait(s_t, ffn_state["mmh"][gl - 3])
                else:
                    y.wait(s_t, s_t.n)
                y.op("dma_start", out=rp[:, :], in_=xn_all[kt * 128:(kt + 1) * 128, k * TC:(k + 1) * TC], inc=(s_dy, 16))
                t.wait(s_dy, s_dy.n)
                if kt == 0:
                    t.wait(s_sc, s_sc.n)
                    t.wait(s_v, s_v.n)
                for q, (wsh, it) in enumerate(((gshv, 2 * itp), (gshv, 2 * itp + 1),
                                               (ushv, 2 * itp), (ushv, 2 * itp + 1))):
                    t.op("matmul", pmh[q], wsh[:, kt, it * 128:(it + 1) * 128], rp[:, :],
                         start=(kt == 0), stop=(kt == 15),
                         inc=(s_t, 1) if q == 3 else None)
                ffn_state["mmh"][gl] = s_t.n
                ffn_state["gl"] = gl + 1
            s.wait(s_t, s_t.n)
            for ii in range(2):
                it = 2 * itp + ii
                s.wait(s_v, s_v.n)
                s.op("activation", out=sgbuf[:, :], in_=pmh[ii], func=AF.Silu, inc=(s_sc, 1))
                v.wait(s_sc, s_sc.n)
                v.op("tensor_tensor", out=inter[:, it, :], in0=pmh[2 + ii], in1=sgbuf[:, :],
                     op=ALU.mult, inc=(s_v, 1))
        for ot in range(16):
            pt = pb1 if ot % 2 == 0 else pb2
            t.wait(s_sc, s_sc.n)
            t.wait(s_v, s_v.n)
            for kt in range(4):
                t.op("matmul", pt[:, :], dshv[:, kt, ot * 128:(ot + 1) * 128], inter[:, kt, :],
                     start=(kt == 0), stop=(kt == 3), inc=(s_t, 1) if kt == 3 else None)
            s.wait(s_t, s_t.n)
            s.wait(s_dg, s_dg.n)
            s.op("copy", out=xftile[:, :], in_=pt[:, :], inc=(s_sc, 1))
            g.wait(s_sc, s_sc.n)
            row0 = (ot // 2) * 512 + 256 + (ot % 2) * 128
            if half == 0:
                g.op("dma_start", out=rs_in[row0:row0 + 128, k * TC:(k + 1) * TC], in_=xftile[:, :], inc=(s_dg, 16))
            else:
                g.op("dma_start", out=rs_in[row0:row0 + 128, k * TC:(k + 1) * TC], in_=xftile[:, :],
                     accum_op=ALU.add, inc=(s_dg, 16))

    def l1_scan_chunk(k):
        y.wait(s_t, s_t.n)
        y.wait(s_dg, dg_xdbl1b)
        y.op("dma_start", out=dlc[:, :], in_=xdbl1b[0:128, k * TC:(k + 1) * TC], inc=(s_dy, 16))
        bcast_build(k, xdbl1b[128:144, :], xdbl1b[144:160, :])
        for i in range(NTILE1):
            t.wait(s_dy, s_dy.n)
            t.wait(s_sc, s_sc.n)
            t.wait(s_dg, ld_l1w)
            t.op("matmul", pg[:, :], wdt1v[:, i, :], dlc[:, :], start=True, stop=True, inc=(s_t, 1))
            s.wait(s_t, s_t.n)
            s.wait(s_v, s_v.n)
            s.op("activation", out=spt1[:, :], in_=pg[:, :], func=AF.Abs, bias=bcol[:, i:i + 1])
            s.op("activation", out=spt2[:, :], in_=spt1[:, :], func=AF.Exp, scale=-1.0)
            s.op("activation", out=spt1[:, :], in_=spt2[:, :], func=AF.Ln, bias=1.0)
            s.op("activation", out=spt2[:, :], in_=pg[:, :], func=AF.Relu, bias=bcol[:, i:i + 1], inc=(s_sc, 1))
            v.wait(s_sc, s_sc.n)
            v.op("tensor_tensor", out=delta[:, :], in0=spt1[:, :], in1=spt2[:, :], op=ALU.add, inc=(s_v, 1))
            delta_done = s_v.n
            y.wait(s_v, s_v.n)
            y.wait(s_dy, s_dy.n)
            y.op("dma_start", out=uch[:, :], in_=xc_dram[i * 128:(i + 1) * 128, k * TC:(k + 1) * TC], inc=(s_dy, 16))
            y.op("dma_start", out=zch_l1[:, :], in_=z_dram[i * 128:(i + 1) * 128, k * TC:(k + 1) * TC], inc=(s_dy, 16))
            s.wait(s_v, delta_done)
            for n in range(DN):
                s.op("activation", out=dA[:, n, :], in_=delta[:, :], func=AF.Exp,
                     scale=Acol[:, i, n:n + 1], inc=(s_sc, 1) if n == DN - 1 else None)
            dA_done = s_sc.n
            v.wait(s_dy, s_dy.n)
            v.op("tensor_tensor", out=duch[:, :], in0=delta[:, :], in1=uch[:, :], op=ALU.mult)
            dub = duch[:, :].unsqueeze(1).broadcast_to([128, DN, TC])
            v.op("tensor_tensor", out=dBub[:, :, :], in0=dub, in1=Bbc[:, :, :], op=ALU.mult)
            v.wait(s_sc, dA_done)
            v.op("tensor_tensor", out=fix16[:, :], in0=dA[:, :, 0], in1=hprev[:, i, :], op=ALU.mult)
            v.op("tensor_tensor", out=dBub[:, :, 0], in0=dBub[:, :, 0], in1=fix16[:, :], op=ALU.add)
            v.op("memset", dA[:, :, 0], 0.0)
            v.op("tensor_tensor_scan", out=Hflat, data0=dAf, data1=dBuflat,
                 initial=0.0, op0=ALU.mult, op1=ALU.add, inc=(s_v, 1))
            v.wait(s_v, s_v.n)
            v.op("tensor_copy", out=hprev[:, i, :], in_=Hb[:, :, TC - 1])
            v.op("tensor_tensor", out=dBuflat, in0=Hflat, in1=Cbcflat, op=ALU.mult)
            v.op("tensor_tensor", out=Hb[:, 0:8, :], in0=dBub[:, 0:8, :], in1=dBub[:, 8:16, :], op=ALU.add)
            v.op("tensor_tensor", out=dBub[:, 0:4, :], in0=Hb[:, 0:4, :], in1=Hb[:, 4:8, :], op=ALU.add)
            v.op("tensor_tensor", out=Hb[:, 0:2, :], in0=dBub[:, 0:2, :], in1=dBub[:, 2:4, :], op=ALU.add)
            v.op("tensor_tensor", out=ych[:, :], in0=Hb[:, 0:1, :].rearrange("p a t -> p (a t)"),
                 in1=Hb[:, 1:2, :].rearrange("p a t -> p (a t)"), op=ALU.add)
            v.op("scalar_tensor_tensor", out=spt2[:, :], in0=uch[:, :], scalar=Dcol[:, i:i + 1],
                 in1=ych[:, :], op0=ALU.mult, op1=ALU.add, inc=(s_v, 1))
            s.wait(s_v, s_v.n)
            s.op("activation", out=zsil[:, :], in_=zch_l1[:, :], func=AF.Silu, inc=(s_sc, 1))
            v.wait(s_sc, s_sc.n)
            v.op("tensor_tensor", out=y3st[:, i, :], in0=spt2[:, :], in1=zsil[:, :],
                 op=ALU.mult, inc=(s_v, 1))
        for ot in range(16):
            pt = pb1 if ot % 2 == 0 else pb2
            t.wait(s_sc, s_sc.n)
            t.wait(s_v, s_v.n)
            for kt in range(NTILE1):
                t.op("matmul", pt[:, :], woutv[:, kt, ot * 128:(ot + 1) * 128], y3st[:, kt, :],
                     start=(kt == 0), stop=(kt == NTILE1 - 1), inc=(s_t, 1) if kt == NTILE1 - 1 else None)
            s.wait(s_t, s_t.n)
            s.wait(s_dy, s_dy.n)
            s.op("copy", out=xtile[:, :], in_=pt[:, :], inc=(s_sc, 1))
            y.wait(s_sc, s_sc.n)
            row0 = (ot // 2) * 512 + (ot % 2) * 128
            y.op("dma_start", out=rs_in[row0:row0 + 128, k * TC:(k + 1) * TC], in_=xtile[:, :], inc=(s_dy, 16))

    zch_l1 = xntile[:, :]

    v.wait(s_v, s_v.n)
    v.op("memset", hprev[:, :, :], 0.0, inc=(s_v, 1))
    ffn_build(0)
    sc_i = 0
    for i in range(32):
        h = 0 if i < 16 else 1
        if i == 16:
            ffn_build(1)
            g.wait(s_dg, s_dg.n)
        ffn_chunk(h, i % 16)
        if i % 2 == 1 and sc_i < NCH:
            l1_scan_chunk(sc_i)
            sc_i += 1
    while sc_i < NCH:
        l1_scan_chunk(sc_i)
        sc_i += 1

    barrier()
    g.wait(s_dy, s_dy.n)
    g.wait(s_dg, s_dg.n)
    g.op("collective_compute", "ReduceScatter", ALU.add,
         replica_groups=[list(range(NC))], ins=[rs_in.opt()],
         outs=[rs_out.opt()], inc=(s_cc, 1))
    cc_rs = s_cc.n
    y.wait(s_cc, cc_rs)
    for k in range(NCH):
        for i in range(NTILE):
            y.wait(s_v, s_v.n)
            y.op("dma_start", out=xtile[:, :], in_=rs_out[i * 128:(i + 1) * 128, k * TC:(k + 1) * TC], inc=(s_dy, 16))
            y.op("dma_start", out=spt1[:, :], in_=rs_out[DSH + i * 128:DSH + (i + 1) * 128, k * TC:(k + 1) * TC], inc=(s_dy, 16))
            y.op("dma_start", out=spt2[:, :], in_=x1_dram[i * 128:(i + 1) * 128, k * TC:(k + 1) * TC], inc=(s_dy, 16))
            s.wait(s_dy, s_dy.n)
            s.wait(s_v, s_v.n)
            s.op("activation", out=delta[:, :], in_=spt1[:, :], func=AF.Silu, inc=(s_sc, 1))
            v.wait(s_sc, s_sc.n)
            v.op("tensor_tensor", out=ych[:, :], in0=xtile[:, :], in1=delta[:, :], op=ALU.mult)
            v.op("tensor_tensor", out=xtile[:, :], in0=spt2[:, :], in1=ych[:, :], op=ALU.add, inc=(s_v, 1))
            y.wait(s_v, s_v.n)
            y.op("dma_start", out=out_sh[i * 128:(i + 1) * 128, k * TC:(k + 1) * TC], in_=xtile[:, :], inc=(s_dy, 16))
    return finish()


_BUILD_CACHE = {}


def kernel(**inputs):
    in_maps = prep_inputs(inputs)
    if "nc" not in _BUILD_CACHE:
        _BUILD_CACHE["nc"] = build()[0]
    nc = _BUILD_CACHE["nc"]
    res = run_bass_kernel_spmd(nc, in_maps, core_ids=list(range(NC)))
    shards = [np.asarray(res.results[c]["out_sh"]) for c in range(NC)]
    full = np.concatenate(shards, axis=0)
    return np.ascontiguousarray(full.T)[None].astype(np.float32)
